# version 11
# baseline (speedup 1.0000x reference)
import numpy as np

import concourse.bass as bass
import concourse.mybir as mybir
from concourse.bass_utils import run_bass_kernel_spmd

F = mybir.dt.float32
BF = mybir.dt.bfloat16
ALU = mybir.AluOpType
ACT = mybir.ActivationFunctionType
AX = mybir.AxisListType.X

B, T, C = 8, 2048, 1024
NG = 8
NQ = 4
TQ = T // NQ
NT = 16
NE = 8
HB = 32
NEG = -1.0e38
EPS = 1.0e-5

DEBUG = False


class Prog:
    ENGS = ("te", "ve", "sc", "sy", "gp")

    def __init__(self):
        self.stream = {e: [] for e in self.ENGS}
        self.sem_names = ["te", "ve", "sc"]
        self.cnt = {s: 0 for s in self.sem_names}
        self.waited = {e: {} for e in self.ENGS}

    def sem(self, name):
        if name not in self.sem_names:
            self.sem_names.append(name)
            self.cnt[name] = 0
        return name

    def op(self, eng, fn, inc=None, n=1):
        tok = None
        if inc is not None:
            self.cnt[inc] += n
            tok = (inc, self.cnt[inc])

        def run(h, sems, fn=fn, inc=inc, n=n):
            r = fn(h)
            if inc is not None:
                r.then_inc(sems[inc], n)
        self.stream[eng].append(run)
        return tok

    def wait(self, eng, tok):
        if tok is None:
            return
        s, v = tok
        if self.waited[eng].get(s, 0) >= v:
            return
        self.waited[eng][s] = v

        def run(h, sems, s=s, v=v):
            h.wait_ge(sems[s], v)
        self.stream[eng].append(run)

    def waits(self, eng, toks):
        for t in toks:
            self.wait(eng, t)

    def bar(self, sem):
        return (sem, self.cnt[sem])


def _tile_lhsT(WT):
    K, M = WT.shape
    return np.ascontiguousarray(
        WT.reshape(K // 128, 128, M // 128, 128).transpose(2, 1, 0, 3)
        .reshape(M // 128, 128, K))


def _cols(vec):
    return np.ascontiguousarray(vec.reshape(8, 128).T)


def host_prep(inputs):
    g = lambda k: np.ascontiguousarray(np.asarray(inputs[k], dtype=np.float32))
    kvr = np.stack([_tile_lhsT(g("L0_attn_Wk").T), _tile_lhsT(g("L0_attn_Wv").T),
                    _tile_lhsT(g("L0_attn_Wr").T)], axis=1)
    wck = _tile_lhsT(g("L0_attn_Wck").T)
    wcv = np.ascontiguousarray(g("L0_attn_Wcv").T.reshape(32, 128, 1024))
    wcr = np.ascontiguousarray(g("L0_attn_Wcr").T.reshape(8, 128, 1024))
    routr = np.ascontiguousarray(g("L0_ffn_router_w").T.reshape(8, 128, 8))
    w1 = g("L0_ffn_w1")
    w1t = np.stack([_tile_lhsT(w1[e].T) for e in range(NE)])
    w2 = g("L0_ffn_w2")
    w2t = np.stack([np.ascontiguousarray(w2[e].T.reshape(16, 128, 1024))
                    for e in range(NE)])
    cols = np.concatenate([
        _cols(g("L0_attn_" + k)) for k in
        ("w", "u", "rk", "rv", "rr", "crk", "crr",
         "ln1_g", "ln1_b", "ln2_g", "ln2_b")], axis=1)
    cols = np.ascontiguousarray(np.pad(cols, ((0, 0), (0, 8))))
    return {
        "kvrw": kvr, "wckw": wck, "wcvw": wcv, "wcrw": wcr, "routw": routr,
        "w1w": w1t, "w2w": w2t, "cols": cols,
        "ident": np.eye(128, dtype=np.float32),
        "ones": np.ones((128, 512), dtype=np.float32),
    }


CW, CU, CRK, CRV, CRR, CCRK, CCRR, CG1, CB1, CG2, CB2 = range(11)


def build_nc(debug=False):
    nc = bass.Bass("TRN2", target_bir_lowering=False)
    p = Prog()

    dx = nc.dram_tensor("x", [T, C], F, kind="ExternalInput").ap()
    dout = nc.dram_tensor("out", [T, C], F, kind="ExternalOutput").ap()
    dkvr = nc.dram_tensor("kvrw", [NG, 3, 128, 1024], F, kind="ExternalInput").ap()
    dwck = nc.dram_tensor("wckw", [HB, 128, 1024], F, kind="ExternalInput").ap()
    dwcv = nc.dram_tensor("wcvw", [HB, 128, 1024], F, kind="ExternalInput").ap()
    dwcr = nc.dram_tensor("wcrw", [8, 128, 1024], F, kind="ExternalInput").ap()
    drout = nc.dram_tensor("routw", [8, 128, 8], F, kind="ExternalInput").ap()
    dw1 = nc.dram_tensor("w1w", [NE, 32, 128, 1024], F, kind="ExternalInput").ap()
    dw2 = nc.dram_tensor("w2w", [NE, 16, 128, 1024], F, kind="ExternalInput").ap()
    dcols = nc.dram_tensor("cols", [128, 96], F, kind="ExternalInput").ap()
    dident = nc.dram_tensor("ident", [128, 128], F, kind="ExternalInput").ap()
    dones = nc.dram_tensor("ones", [128, 512], F, kind="ExternalInput").ap()
    taps = {}
    if debug:
        for nm, w in (("tap_x1", NT * 1024), ("tap_mix", NG * TQ),
                      ("tap_xfbf", NG * T), ("tap_comb", 128),
                      ("tap_a", NT * 1024), ("tap_b", NG * TQ),
                      ("tap_c", NT * 1024)):
            taps[nm] = nc.dram_tensor(nm, [128, w], F, kind="ExternalOutput").ap()

    ctx = []
    def ent(x):
        ctx.append(x)
        return x.__enter__()

    TO = 256
    X1 = ent(nc.sbuf_tensor("X1", [128, NT * 1024], F))
    VBp = [ent(nc.sbuf_tensor("VB%d" % i, [128, TO], F)) for i in range(2)]
    EKBp = [ent(nc.sbuf_tensor("EKB%d" % i, [128, TO], F)) for i in range(2)]
    ABp = [ent(nc.sbuf_tensor("AB%d" % i, [128, TO], F)) for i in range(2)]
    T1p = [ent(nc.sbuf_tensor("T1_%d" % i, [128, TO], F)) for i in range(2)]
    T2p = [ent(nc.sbuf_tensor("T2_%d" % i, [128, TO], F)) for i in range(2)]
    T3p = [ent(nc.sbuf_tensor("T3_%d" % i, [128, TO], F)) for i in range(2)]
    RBp = [ent(nc.sbuf_tensor("RB%d" % i, [128, TO], F)) for i in range(2)]
    YBp = [ent(nc.sbuf_tensor("YB%d" % i, [128, TO], F)) for i in range(2)]
    PENp = [ent(nc.sbuf_tensor("PEN%d" % i, [128, TO + 4], F)) for i in range(2)]
    AABp = [ent(nc.sbuf_tensor("AAB%d" % i, [128, TO + 4], F)) for i in range(2)]
    SHB = ent(nc.sbuf_tensor("SHB", [128, NG * 132], F))
    XT0 = ent(nc.sbuf_tensor("XT0", [128, 1024], F))
    XT1 = ent(nc.sbuf_tensor("XT1", [128, 1024], F))
    XTILEp = [ent(nc.sbuf_tensor("XTILE%d" % i, [128, 1024], F)) for i in range(2)]
    SCR = ent(nc.sbuf_tensor("SCR", [128, 1024], F))
    SCR2 = ent(nc.sbuf_tensor("SCR2", [128, 1024], F))
    VCB = ent(nc.sbuf_tensor("VCB", [128, 1024], F))
    CS = ent(nc.sbuf_tensor("CS", [128, 16], F))
    COLS = ent(nc.sbuf_tensor("COLS", [128, 96], F))
    DCOL = ent(nc.sbuf_tensor("DCOL", [128, 64], F))
    IDENT = ent(nc.sbuf_tensor("IDENT", [128, 128], F))
    ROUTW = ent(nc.sbuf_tensor("ROUTW", [128, 64], F))
    CMB = ent(nc.sbuf_tensor("CMB", [128, 128], F))
    SM8 = ent(nc.sbuf_tensor("SM8", [128, 64], F))
    KSCR = ent(nc.sbuf_tensor("KSCR", [128, 256], F))

    BFA = ent(nc.sbuf_tensor("BFA", [128, 45056], BF))
    MIXA = BFA[:, 0:4096]
    MIXB = BFA[:, 4096:8192]
    MIXC = BFA[:, 8192:12288]
    KVRW = [BFA[:, 12288 + 3072 * i:12288 + 3072 * (i + 1)] for i in range(2)]
    WCRW = BFA[:, 12288 + 6144:12288 + 6144 + 8192]
    WCKW = [BFA[:, 26624 + 4096 * i:26624 + 4096 * (i + 1)] for i in range(2)]
    WCVW = [BFA[:, 34816 + 4096 * i:34816 + 4096 * (i + 1)] for i in range(2)]
    KCB = BFA[:, 43008 + 0:43008 + 0]
    KCB = BFA[:, 8192 + 0:8192 + 8192]
    XFBF = BFA[:, 0:16384]
    YBF = BFA[:, 16384:32768]
    W1G = [BFA[:, 32768 + 2048 * i:32768 + 2048 * (i + 1)] for i in range(2)]
    W1U = [BFA[:, 36864 + 2048 * i:36864 + 2048 * (i + 1)] for i in range(2)]
    W2W = [BFA[:, 40960 + 2048 * i:40960 + 2048 * (i + 1)] for i in range(2)]

    PSA = ent(nc.psum_tensor("PSA", [128, 1024], F))
    PSB = ent(nc.psum_tensor("PSB", [128, 1024], F))
    PSC = ent(nc.psum_tensor("PSC", [128, 1024], F))
    PSD = ent(nc.psum_tensor("PSD", [128, 1024], F))
    PS = (PSA, PSB, PSC, PSD)

    col = lambda t, j: t[:, j:j + 1]

    def bc0(colap, n):
        return bass.AP(colap.tensor, colap.offset, [[colap.ap[0][0], 128], [0, n]])
    EW = lambda g: col(DCOL, g)
    EU = lambda g: col(DCOL, 8 + g)
    PP0 = lambda g: col(DCOL, 16 + g)
    AAC = lambda g: col(DCOL, 24 + g)
    MCp = lambda par: col(DCOL, 32 + 6 * par)
    MNp = lambda par: col(DCOL, 33 + 6 * par)
    C0p = lambda par: col(DCOL, 34 + 6 * par)
    C1p = lambda par: col(DCOL, 35 + 6 * par)
    KMp = lambda par: col(DCOL, 36 + 6 * par)
    MC, MN, C0, C1 = MCp(0), MNp(0), C0p(0), C1p(0)
    ccol = lambda name, j: col(COLS, 8 * name + j)
    xtile = lambda tt: X1[:, 1024 * tt:1024 * (tt + 1)]

    p.sem("dcn")
    p.op("sy", lambda h: h.dma_start(COLS[:, :], dcols), inc="dcn", n=16)
    p.op("sy", lambda h: h.dma_start(IDENT[:, :], dident), inc="dcn", n=16)
    t_const = p.op("sy", lambda h: h.dma_start(
        ROUTW[:, :], bass.AP(drout.tensor, 0, [[8, 128], [128 * 8, 8], [1, 8]])), inc="dcn", n=16)
    t_cols = t_id = t_ones = t_rout = t_const
    p.wait("sc", t_const)
    t_ew = p.op("sc", lambda h: h.activation(DCOL[:, 0:8], COLS[:, 8 * CW:8 * CW + 8], ACT.Exp), inc="sc")
    t_eu = p.op("sc", lambda h: h.activation(DCOL[:, 8:16], COLS[:, 8 * CU:8 * CU + 8], ACT.Exp), inc="sc")
    t_pp0 = p.op("ve", lambda h: h.memset(DCOL[:, 16:24], NEG), inc="ve")
    t_eps = p.op("ve", lambda h: h.memset(DCOL[:, 56:57], EPS), inc="ve")
    t_aac = p.op("ve", lambda h: h.memset(DCOL[:, 24:32], 0.0), inc="ve")
    t_shb = p.op("ve", lambda h: h.memset(SHB[:, :], 0.0), inc="ve")
    p.waits("ve", [t_ew, t_eu, t_id, t_ones, t_rout])

    last_sq = [None, None]

    psfree = {
        (i, hh): None for i in range(4) for hh in range(2)
    }

    def transpose_to(slot_ti, slot_half, slot_pos, src_ap, extra=()):
        slot = PS[slot_ti][:, 512 * slot_half + 128 * slot_pos:
                           512 * slot_half + 128 * slot_pos + 128]
        p.waits("te", [psfree[(slot_ti, slot_half)]] + list(extra))
        t = p.op("te", lambda h, slot=slot: h.transpose(slot, src_ap, IDENT[:, :]), inc="te")
        return slot, t

    mix_tok = [None] * NQ
    x1_tok = [None] * NT
    xt_free = [(None, None), (None, None)]
    shb_tok = [None] * NG
    xtile_free = [None, None]
    shb_tok = [None] * NG
    xtile_free = [None, None]

    def runz(*gens, skew=0):
        gens = [g for g in gens if g is not None]
        if skew and len(gens) > 1:
            for j, g in enumerate(list(gens)):
                for _ in range((len(gens) - 1 - j) * skew):
                    try:
                        next(g)
                    except StopIteration:
                        gens.remove(g)
                        break
        while gens:
            for g in list(gens):
                try:
                    next(g)
                except StopIteration:
                    gens.remove(g)

    def g_ln(src_ap, dst_ap, par, dep_ve=(), dep_sc=(), out=None):
        b = 8 * par
        mu_ap, r_ap = col(CS, b + 5), col(CS, b + 6)
        p.waits("ve", list(dep_ve))
        p.waits("sc", list(dep_sc) + [last_sq[par]])
        t_s = p.op("ve", lambda h: h.reduce_sum(col(CS, b + 0), src_ap, axis=AX), inc="ve")
        yield
        t_q = p.op("sc", lambda h: h.activation(dst_ap, src_ap, ACT.Square,
                                                accum_out=col(CS, b + 1)), inc="sc")
        last_sq[par] = t_q
        yield
        p.waits("ve", [t_s, t_q])
        t_mu = p.op("ve", lambda h: h.tensor_scalar(mu_ap, col(CS, b + 0), 1.0 / C, None, ALU.mult), inc="ve")
        yield
        p.wait("ve", t_mu)
        t_m2 = p.op("ve", lambda h: h.tensor_tensor(col(CS, b + 2), mu_ap, mu_ap, ALU.mult), inc="ve")
        yield
        p.wait("ve", t_m2)
        t_va = p.op("ve", lambda h: h.scalar_tensor_tensor(
            col(CS, b + 3), col(CS, b + 1), 1.0 / C, col(CS, b + 2), ALU.mult, ALU.subtract), inc="ve")
        yield
        p.wait("sc", t_va)
        t_ln = p.op("sc", lambda h: h.activation(col(CS, b + 4), col(CS, b + 3), ACT.Ln, bias=col(DCOL, 56)), inc="sc")
        yield
        p.wait("sc", t_ln)
        t_r = p.op("sc", lambda h: h.activation(r_ap, col(CS, b + 4), ACT.Exp, scale=-0.5), inc="sc")
        yield
        p.wait("ve", t_r)
        t_d = p.op("ve", lambda h: h.tensor_scalar(dst_ap, src_ap, mu_ap, r_ap,
                                                   ALU.subtract, ALU.mult), inc="ve")
        if out is not None:
            out["d"] = t_d
            out["q"] = t_q

    def g_p12tile(tt):
        q, tl = tt // 4, tt % 4
        par = tt % 2
        xt = XT0 if par == 0 else XT1
        p.waits("sy", list(xt_free[par]))
        t_x = p.op("sy", lambda h, tt=tt, xt=xt: h.dma_start(
            xt[:, :], dx[128 * tt:128 * (tt + 1), :]), inc=p.sem("dx%d" % par), n=16)
        yield
        o1 = {}
        for _ in g_ln(xt[:, :], xtile(tt), par, dep_ve=[t_x, t_shb], dep_sc=[t_x], out=o1):
            yield
        t_xln, t_sq = o1["d"], o1["q"]
        xt_free[par] = (t_xln, t_sq)
        x1_tok[tt] = t_xln
        XTILE = XTILEp[par]
        o2 = {}
        for _ in g_ln(xtile(tt), XTILE[:, :], par, dep_ve=[t_xln, xtile_free[par]],
                      dep_sc=[t_xln, xtile_free[par]], out=o2):
            yield
        t_xa = o2["d"]
        last = None
        for g in range(NG):
            slot, t_tr = transpose_to(0, par, g % 4, XTILE[:, 128 * g:128 * (g + 1)],
                                      extra=[t_xa])
            xtile_free[par] = t_tr
            yield
            sh = SHB[:, 132 * g:132 * g + 132]
            dscr = SCR2[:, 128 * (4 * par + g % 4):128 * (4 * par + g % 4) + 128]
            p.waits("ve", [t_tr, shb_tok[g]])
            t_ap = p.op("ve", lambda h, sh=sh, slot=slot, g=g: h.tensor_scalar(
                sh[:, 1:129], slot, ccol(CG1, g), ccol(CB1, g), ALU.mult, ALU.add), inc="ve")
            psfree[(0, par)] = t_ap
            yield
            p.wait("ve", t_ap)
            t_db = p.op("ve", lambda h, sh=sh, dscr=dscr: h.tensor_tensor(
                dscr, sh[:, 1:129], sh[:, 0:128], ALU.subtract), inc="ve")
            yield
            p.wait("ve", t_db)
            t_mx = None
            for mi, (dstT, rc) in enumerate(((MIXA, CRK), (MIXB, CRV), (MIXC, CRR))):
                dst = dstT[:, 512 * g + 128 * tl:512 * g + 128 * (tl + 1)]
                t_mx = p.op("ve", lambda h, dst=dst, rc=rc, sh=sh, g=g, dscr=dscr: h.scalar_tensor_tensor(
                    dst, dscr, ccol(rc, g), sh[:, 0:128], ALU.mult, ALU.add), inc="ve")
                yield
            p.wait("ve", t_mx)
            t_bd = p.op("ve", lambda h, sh=sh: h.tensor_copy(sh[:, 0:1], sh[:, 128:129]), inc="ve")
            shb_tok[g] = t_bd
            yield
            last = t_bd
        if mix_tok[q] is None or last[1] > mix_tok[q][1]:
            mix_tok[q] = last

    def kvr_load(i, g, dep=None):
        p.wait("gp", dep)
        return p.op("gp", lambda h, i=i, g=g: h.dma_start(
            KVRW[i], bass.AP(dkvr.tensor, g * 3 * 128 * 1024,
                             [[1024, 128], [128 * 1024, 3], [1, 1024]])), inc=p.sem("kvr%d" % i), n=16)

    kvr_dma = [None, None]
    kvr_free = [None, None]

    def g_p3(ii, iters):
        q, oj, g = iters[ii]
        par = ii % 2
        slot = KVRW[par]
        t_w = kvr_dma[par]
        ob = TO * oj
        mix = mix_tok[q]
        MC, MN, C0, C1, KM = MCp(par), MNp(par), C0p(par), C1p(par), KMp(par)
        EKB, VB, RB, YB = EKBp[par], VBp[par], RBp[par], YBp[par]
        T1, T2, T3, AB = T1p[par], T2p[par], T3p[par], ABp[par]
        PEN, AAB = PENp[par], AABp[par]
        psl = lambda t: t[:, 512 * par:512 * par + TO]
        p.waits("te", [mix, t_w, psfree[(1, par)]])
        tk = None
        for ck in range(NG):
            tk = p.op("te", lambda h, ck=ck, slot=slot, par=par, ob=ob: h.matmul(
                PSB[:, 512 * par:512 * par + TO], slot[:, 128 * ck:128 * (ck + 1)],
                MIXA[:, 512 * ck + ob:512 * ck + ob + TO], start=(ck == 0), stop=(ck == 7)),
                inc="te" if ck == 7 else None)
        yield
        p.wait("te", psfree[(2, par)])
        tv = None
        for ck in range(NG):
            tv = p.op("te", lambda h, ck=ck, slot=slot, par=par, ob=ob: h.matmul(
                PSC[:, 512 * par:512 * par + TO], slot[:, 1024 + 128 * ck:1024 + 128 * (ck + 1)],
                MIXB[:, 512 * ck + ob:512 * ck + ob + TO], start=(ck == 0), stop=(ck == 7)),
                inc="te" if ck == 7 else None)
        yield
        p.wait("te", psfree[(3, par)])
        tr = None
        for ck in range(NG):
            tr = p.op("te", lambda h, ck=ck, slot=slot, par=par, ob=ob: h.matmul(
                PSD[:, 512 * par:512 * par + TO], slot[:, 2048 + 128 * ck:2048 + 128 * (ck + 1)],
                MIXC[:, 512 * ck + ob:512 * ck + ob + TO], start=(ck == 0), stop=(ck == 7)),
                inc="te" if ck == 7 else None)
        kvr_free[par] = tr
        if ii + 2 < len(iters):
            kvr_dma[par] = kvr_load(par, iters[ii + 2][2], dep=tr)
        yield
        p.wait("ve", tk)
        t_km = p.op("ve", lambda h: h.reduce_max(KM, psl(PSB), axis=AX), inc="ve")
        yield
        p.wait("ve", t_km)
        t_M = p.op("ve", lambda h, g=g: h.tensor_tensor(MC, KM, PP0(g), ALU.max), inc="ve")
        yield
        p.wait("ve", t_M)
        t_MN = p.op("ve", lambda h: h.tensor_scalar(MN, MC, -1.0, None, ALU.mult), inc="ve")
        yield
        p.waits("sc", [tk, t_MN])
        t_ek = p.op("sc", lambda h: h.activation(EKB[:, :], psl(PSB), ACT.Exp, bias=MN), inc="sc")
        psfree[(1, par)] = t_ek
        yield
        p.wait("sc", tv)
        t_vev = p.op("sc", lambda h: h.copy(VB[:, :], psl(PSC)), inc="sc")
        psfree[(2, par)] = t_vev
        yield
        p.wait("sc", tr)
        t_r1 = p.op("sc", lambda h: h.activation(RB[:, :], psl(PSD), ACT.Exp, scale=-1.0), inc="sc")
        psfree[(3, par)] = t_r1
        yield
        p.wait("ve", t_r1)
        t_r2 = p.op("ve", lambda h: h.tensor_scalar(T3[:, :], RB[:, :], 1.0, None, ALU.add), inc="ve")
        yield
        p.wait("sc", t_r2)
        t_r3 = p.op("sc", lambda h: h.activation(T2[:, :], T3[:, :], ACT.Ln), inc="sc")
        yield
        p.wait("sc", t_r3)
        t_r4 = p.op("sc", lambda h: h.activation(RB[:, :], T2[:, :], ACT.Exp, scale=-1.0), inc="sc")
        yield
        p.waits("ve", [t_ek, t_ew, t_eu])
        t_d1 = p.op("ve", lambda h, g=g: h.tensor_scalar(T1[:, :], EKB[:, :], EW(g), None, ALU.mult), inc="ve")
        yield
        p.wait("sc", t_d1)
        t_pi = p.op("sc", lambda h, g=g: h.activation(PEN[:, 0:1], PP0(g), ACT.Exp, bias=MN), inc="sc")
        yield
        p.waits("ve", [t_d1, t_pi])
        t_ps = p.op("ve", lambda h, g=g: h.tensor_tensor_scan(
            PEN[:, 1:TO + 1], bc0(EW(g), TO), T1[:, :], PEN[:, 0:1], ALU.mult, ALU.add), inc="ve")
        yield
        p.wait("sc", t_ps)
        t_lp = p.op("sc", lambda h: h.activation(C1, PEN[:, TO:TO + 1], ACT.Ln), inc="sc")
        yield
        p.wait("ve", t_lp)
        t_pp = p.op("ve", lambda h, g=g: h.tensor_tensor(PP0(g), C1, MC, ALU.add), inc="ve")
        yield
        t_em = p.op("ve", lambda h: h.tensor_tensor(T2[:, :], PEN[:, 0:TO], EKB[:, :], ALU.max), inc="ve")
        yield
        p.wait("sc", t_em)
        t_l1 = p.op("sc", lambda h: h.activation(T3[:, :], T2[:, :], ACT.Ln), inc="sc")
        yield
        p.wait("sc", t_l1)
        t_iv = p.op("sc", lambda h: h.activation(T2[:, :], T3[:, :], ACT.Exp, scale=-1.0), inc="sc")
        yield
        p.wait("ve", t_iv)
        t_al = p.op("ve", lambda h: h.tensor_tensor(AB[:, :], PEN[:, 0:TO], T2[:, :], ALU.mult), inc="ve")
        yield
        t_be = p.op("ve", lambda h: h.tensor_tensor(T1[:, :], EKB[:, :], T2[:, :], ALU.mult), inc="ve")
        yield
        p.waits("ve", [t_be, t_vev])
        t_bv = p.op("ve", lambda h: h.tensor_tensor(T3[:, :], T1[:, :], VB[:, :], ALU.mult), inc="ve")
        yield
        t_a0 = p.op("ve", lambda h, g=g: h.tensor_copy(AAB[:, 0:1], AAC(g)), inc="ve")
        yield
        p.waits("ve", [t_al, t_bv, t_a0])
        t_as = p.op("ve", lambda h: h.tensor_tensor_scan(
            AAB[:, 1:TO + 1], AB[:, :], T3[:, :], AAB[:, 0:1], ALU.mult, ALU.add), inc="ve")
        yield
        p.wait("ve", t_as)
        t_ac = p.op("ve", lambda h, g=g: h.tensor_copy(AAC(g), AAB[:, TO:TO + 1]), inc="ve")
        yield
        t_Et = p.op("ve", lambda h, g=g: h.tensor_scalar(T1[:, :], EKB[:, :], EU(g), None, ALU.mult), inc="ve")
        yield
        p.wait("ve", t_Et)
        t_n1 = p.op("ve", lambda h: h.tensor_tensor(T2[:, :], PEN[:, 0:TO], AAB[:, 0:TO], ALU.mult), inc="ve")
        yield
        t_n2 = p.op("ve", lambda h: h.tensor_tensor(EKB[:, :], T1[:, :], VB[:, :], ALU.mult), inc="ve")
        yield
        p.waits("ve", [t_n1, t_n2])
        t_nu = p.op("ve", lambda h: h.tensor_tensor(T2[:, :], T2[:, :], EKB[:, :], ALU.add), inc="ve")
        yield
        t_de = p.op("ve", lambda h: h.tensor_tensor(T1[:, :], PEN[:, 0:TO], T1[:, :], ALU.add), inc="ve")
        yield
        p.waits("sc", [t_de])
        t_l2 = p.op("sc", lambda h: h.activation(T3[:, :], T1[:, :], ACT.Ln), inc="sc")
        yield
        p.wait("sc", t_l2)
        t_rd = p.op("sc", lambda h: h.activation(T1[:, :], T3[:, :], ACT.Exp, scale=-1.0), inc="sc")
        yield
        p.waits("ve", [t_nu, t_rd, t_r4])
        t_y = p.op("ve", lambda h: h.tensor_tensor(YB[:, :], T2[:, :], T1[:, :], ALU.mult), inc="ve")
        yield
        p.wait("ve", t_y)
        t_sy = p.op("ve", lambda h: h.tensor_tensor(YB[:, :], YB[:, :], RB[:, :], ALU.mult), inc="ve")
        yield
        for b in range(2):
            tt = 2 * (2 * q + oj) + b
            slot2, t_tr = transpose_to(0, par, g % 4, YB[:, 128 * b:128 * (b + 1)],
                                       extra=[t_sy])
            yield
            dst = X1[:, 1024 * tt + 128 * g:1024 * tt + 128 * (g + 1)]
            p.waits("ve", [t_tr, x1_tok[tt]])
            t_x1 = p.op("ve", lambda h, dst=dst, slot2=slot2: h.tensor_tensor(
                dst, dst, slot2, ALU.add), inc="ve")
            psfree[(0, par)] = t_x1
            x1_tok[tt] = t_x1
            yield

    iters = [(q, oj, g) for q in range(NQ) for oj in range(2) for g in range(NG)]
    kvr_dma[0] = kvr_load(0, iters[0][2])
    kvr_dma[1] = kvr_load(1, iters[1][2])
    for q in range(NQ):
        runz(g_p12tile(4 * q + 0), g_p12tile(4 * q + 1), skew=10)
        runz(g_p12tile(4 * q + 2), g_p12tile(4 * q + 3), skew=10)
        base = q * 16
        for j in range(8):
            runz(g_p3(base + 2 * j, iters), g_p3(base + 2 * j + 1, iters), skew=12)

    if debug:
        bv = p.bar("ve")
        p.wait("sy", bv)
        t_tap = p.op("sy", lambda h: h.dma_start(taps["tap_a"], X1[:, :]), inc=p.sem("dtp"), n=16)
        p.wait("ve", t_tap)
        p.wait("gp", bv)
        t_tap = p.op("gp", lambda h: h.dma_start(taps["tap_b"], MIXA[:, :]), inc=p.sem("dtg"), n=16)
        p.wait("ve", t_tap)
        p.wait("te", t_tap)

    p.wait("ve", p.bar("ve"))
    t_shb2 = p.op("ve", lambda h: h.memset(SHB[:, :], 0.0), inc="ve")
    mix2_tok = [None] * NQ

    def g_p4tile(tt):
        q, tl = tt // 4, tt % 4
        par = tt % 2
        xt = XT0 if par == 0 else XT1
        p.waits("sy", list(xt_free[par]))
        t_x = p.op("sy", lambda h, tt=tt, xt=xt: h.dma_start(
            xt[:, :], dx[128 * tt:128 * (tt + 1), :]), inc=p.sem("dx%d" % par), n=16)
        yield
        XTILE = XTILEp[par]
        o = {}
        for _ in g_ln(xtile(tt), XTILE[:, :], par,
                      dep_ve=[t_shb2, x1_tok[tt], xtile_free[par]],
                      dep_sc=[x1_tok[tt], xtile_free[par]], out=o):
            yield
        t_xb, t_sq = o["d"], o["q"]
        p.waits("ve", [t_xb, t_x])
        t_x2a = p.op("ve", lambda h, tt=tt, xt=xt: h.tensor_tensor(
            xtile(tt), xtile(tt), xt[:, :], ALU.add), inc="ve")
        x1_tok[tt] = t_x2a
        xt_free[par] = (t_x2a, t_sq)
        yield
        last = None
        for g in range(NG):
            slot, t_tr = transpose_to(0, par, g % 4, XTILE[:, 128 * g:128 * (g + 1)],
                                      extra=[t_xb])
            xtile_free[par] = t_tr
            yield
            sh = SHB[:, 132 * g:132 * g + 132]
            dscr = SCR2[:, 128 * (4 * par + g % 4):128 * (4 * par + g % 4) + 128]
            p.waits("ve", [t_tr, shb_tok[g]])
            t_ap = p.op("ve", lambda h, sh=sh, slot=slot, g=g: h.tensor_scalar(
                sh[:, 1:129], slot, ccol(CG2, g), ccol(CB2, g), ALU.mult, ALU.add), inc="ve")
            psfree[(0, par)] = t_ap
            yield
            p.wait("ve", t_ap)
            t_db = p.op("ve", lambda h, sh=sh, dscr=dscr: h.tensor_tensor(
                dscr, sh[:, 1:129], sh[:, 0:128], ALU.subtract), inc="ve")
            yield
            p.wait("ve", t_db)
            t_mx = None
            for dstT, rc in ((MIXA, CCRK), (MIXB, CCRR)):
                dst = dstT[:, 512 * g + 128 * tl:512 * g + 128 * (tl + 1)]
                t_mx = p.op("ve", lambda h, dst=dst, rc=rc, sh=sh, g=g, dscr=dscr: h.scalar_tensor_tensor(
                    dst, dscr, ccol(rc, g), sh[:, 0:128], ALU.mult, ALU.add), inc="ve")
                yield
            p.wait("ve", t_mx)
            t_bd = p.op("ve", lambda h, sh=sh: h.tensor_copy(sh[:, 0:1], sh[:, 128:129]), inc="ve")
            shb_tok[g] = t_bd
            yield
            last = t_bd
        if mix2_tok[q] is None or last[1] > mix2_tok[q][1]:
            mix2_tok[q] = last

    def emit_p4(q):
        runz(g_p4tile(4 * q + 0), g_p4tile(4 * q + 1), skew=10)
        runz(g_p4tile(4 * q + 2), g_p4tile(4 * q + 3), skew=10)

    t_wcr = p.op("gp", lambda h: h.dma_start(
        WCRW, bass.AP(dwcr.tensor, 0, [[1024, 128], [128 * 1024, 8], [1, 1024]])), inc=p.sem("wcr"), n=16)

    def wseq_load(dst_ring, i, dram, hb4, dep=None, tag=""):
        p.wait("gp", dep)
        return p.op("gp", lambda h, i=i, hb4=hb4: h.dma_start(
            dst_ring[i], bass.AP(dram.tensor, hb4 * 4 * 128 * 1024,
                                 [[1024, 128], [128 * 1024, 4], [1, 1024]])),
            inc=p.sem("%s%d" % (tag, i)), n=16)

    wck_dma = [None, None]
    wcv_dma = [None, None]
    wck_free = [None, None]
    wcv_free = [None, None]
    kc_tok = [None] * HB
    vcb_free = None

    def emit_p5(oct_):
        nonlocal vcb_free
        q, oj = oct_ // 2, oct_ % 2
        mix = mix2_tok[q]
        ob = 256 * oj
        wck_dma[0] = wseq_load(WCKW, 0, dwck, 0, dep=wck_free[0], tag='wck')
        for hb4 in range(8):
            i = hb4 % 2
            if hb4 + 1 < 8:
                wck_dma[(i + 1) % 2] = wseq_load(WCKW, (i + 1) % 2, dwck, hb4 + 1,
                                                 dep=wck_free[(i + 1) % 2], tag='wck')
            p.waits("te", [mix, wck_dma[i]])
            for hj in range(4):
                hblk = 4 * hb4 + hj
                half = hblk % 2
                p.wait("te", psfree[(2, half)])
                t_mm = None
                for ck in range(NG):
                    t_mm = p.op("te", lambda h, hj=hj, ck=ck, half=half, i=i, ob=ob: h.matmul(
                        PSC[:, 512 * half + 0:512 * half + 256],
                        WCKW[i][:, 1024 * hj + 128 * ck:1024 * hj + 128 * (ck + 1)],
                        MIXA[:, 512 * ck + ob:512 * ck + ob + 256],
                        start=(ck == 0), stop=(ck == 7)), inc="te" if ck == 7 else None)
                p.wait("sc", t_mm)
                t_rl = p.op("sc", lambda h, half=half: h.activation(
                    KSCR[:, :], PSC[:, 512 * half:512 * half + 256], ACT.Relu), inc="sc")
                p.waits("ve", [t_rl, t_mm])
                t_kc = p.op("ve", lambda h, hblk=hblk, half=half: h.tensor_tensor(
                    KCB[:, 256 * hblk:256 * (hblk + 1)], KSCR[:, :],
                    PSC[:, 512 * half:512 * half + 256], ALU.mult), inc="ve")
                p.wait("sc", t_kc)
                kc_tok[hblk] = t_kc
                psfree[(2, half)] = t_kc
                if hj == 3:
                    wck_free[i] = t_mm
        wcv_dma[0] = wseq_load(WCVW, 0, dwcv, 0, dep=wcv_free[0], tag='wcv')
        t_vc = [None, None]
        for hb4 in range(8):
            i = hb4 % 2
            if hb4 + 1 < 8:
                wcv_dma[(i + 1) % 2] = wseq_load(WCVW, (i + 1) % 2, dwcv, hb4 + 1,
                                                 dep=wcv_free[(i + 1) % 2], tag='wcv')
            p.wait("te", wcv_dma[i])
            for hj in range(4):
                hblk = 4 * hb4 + hj
                p.wait("te", kc_tok[hblk])
                if hblk == 0:
                    p.waits("te", [psfree[(0, 0)], psfree[(0, 1)],
                                   psfree[(1, 0)], psfree[(1, 1)]])
                t_last = None
                for t2 in range(2):
                    for dh in range(2):
                        t_last = p.op("te", lambda h, hblk=hblk, t2=t2, dh=dh, i=i, hj=hj: h.matmul(
                            PS[t2][:, 512 * dh:512 * (dh + 1)],
                            KCB[:, 256 * hblk + 128 * t2:256 * hblk + 128 * (t2 + 1)],
                            WCVW[i][:, 1024 * hj + 512 * dh:1024 * hj + 512 * (dh + 1)],
                            start=(hblk == 0), stop=(hblk == HB - 1)),
                            inc="te" if (hblk == HB - 1 and t2 == 1 and dh == 1) or (hj == 3 and t2 == 1 and dh == 1) else None)
                t_vc[1] = t_last
            if hb4 == 7:
                wcv_free[i] = t_last
            else:
                wcv_free[i] = t_last
        t_vc_done = t_vc[1]
        for t2 in range(2):
            tt = 2 * oct_ + t2
            p.waits("ve", [t_vc_done, vcb_free])
            t_ev = p.op("ve", lambda h, t2=t2: h.tensor_copy(VCB[:, :], PS[t2][:, :]), inc="ve")
            psfree[(t2, 0)] = t_ev
            psfree[(t2, 1)] = t_ev
            p.waits("te", [mix, t_wcr, psfree[(3, 0)], psfree[(3, 1)]])
            t_cr = None
            for ck in range(NG):
                for dh in range(2):
                    t_cr = p.op("te", lambda h, ck=ck, dh=dh, t2=t2, ob=ob: h.matmul(
                        PSD[:, 512 * dh:512 * (dh + 1)],
                        MIXB[:, 512 * ck + ob + 128 * t2:512 * ck + ob + 128 * (t2 + 1)],
                        WCRW[:, 1024 * ck + 512 * dh:1024 * ck + 512 * (dh + 1)],
                        start=(ck == 0), stop=(ck == 7)),
                        inc="te" if (ck == 7 and dh == 1) else None)
            p.wait("sc", t_cr)
            t_s1 = p.op("sc", lambda h: h.activation(SCR[:, :], PSD[:, :], ACT.Exp, scale=-1.0), inc="sc")
            psfree[(3, 0)] = t_s1
            psfree[(3, 1)] = t_s1
            p.wait("ve", t_s1)
            t_s2 = p.op("ve", lambda h: h.tensor_scalar(SCR2[:, :], SCR[:, :], 1.0, None, ALU.add), inc="ve")
            p.wait("sc", t_s2)
            t_s3 = p.op("sc", lambda h: h.activation(SCR[:, :], SCR2[:, :], ACT.Ln), inc="sc")
            p.wait("sc", t_s3)
            t_s4 = p.op("sc", lambda h: h.activation(SCR2[:, :], SCR[:, :], ACT.Exp, scale=-1.0), inc="sc")
            p.wait("sc", t_s4)
            p.waits("ve", [t_s4, t_ev])
            t_m = p.op("ve", lambda h: h.tensor_tensor(VCB[:, :], SCR2[:, :], VCB[:, :], ALU.mult), inc="ve")
            p.waits("ve", [t_m, x1_tok[tt]])
            t_x2 = p.op("ve", lambda h, tt=tt: h.tensor_tensor(
                xtile(tt), xtile(tt), VCB[:, :], ALU.add), inc="ve")
            x1_tok[tt] = t_x2
            vcb_free = t_x2


    for q in range(NQ):
        emit_p4(q)
        if debug and q == NQ - 1:
            bv = p.bar("ve")
            p.wait("gp", bv)
            t_tapm = p.op("gp", lambda h: h.dma_start(taps["tap_mix"], MIXA[:, :]), inc=p.sem("dtg"), n=16)
            p.wait("ve", t_tapm)
            p.wait("te", t_tapm)
        emit_p5(2 * q)
        emit_p5(2 * q + 1)
    t_p5te_glob = p.bar("te")

    if debug:
        bv = p.bar("ve")
        p.wait("sy", bv)
        t_tap = p.op("sy", lambda h: h.dma_start(taps["tap_c"], X1[:, :]), inc=p.sem("dtp"), n=16)
        p.wait("ve", t_tap)

    tr6 = {0: p.bar("ve"), 1: p.bar("ve")}
    prev_router = {0: None, 1: None}
    strip_bufs = (SCR2, VCB)
    cmb_tok = [None] * NT

    def g_p6tile(tt):
        par = tt % 2
        sb = lambda j: col(DCOL, 44 + 6 * par + j)
        smb = 32 * par
        STRIP = strip_bufs[par]
        XTILE = XTILEp[par]
        o = {}
        for _ in g_ln(xtile(tt), XTILE[:, :], par,
                      dep_ve=[x1_tok[tt], xtile_free[par]],
                      dep_sc=[x1_tok[tt], xtile_free[par]], out=o):
            yield
        t_xf = o["d"]
        strip_done = None
        for g in range(NG):
            ti = 1 + g // 4
            slot, t_tr = transpose_to(ti, par, g % 4, XTILE[:, 128 * g:128 * (g + 1)],
                                      extra=[t_xf, prev_router[par]])
            xtile_free[par] = t_tr
            yield
            p.waits("ve", [t_tr, prev_router[par]])
            t_c1 = p.op("ve", lambda h, g=g, slot=slot, STRIP=STRIP: h.tensor_copy(
                STRIP[:, 128 * g:128 * (g + 1)], slot), inc="ve")
            psfree[(ti, par)] = t_c1
            yield
            p.wait("ve", t_c1)
            t_c2 = p.op("ve", lambda h, g=g, tt=tt, STRIP=STRIP: h.tensor_copy(
                XFBF[:, 2048 * g + 128 * tt:2048 * g + 128 * (tt + 1)],
                STRIP[:, 128 * g:128 * (g + 1)]), inc="ve")
            strip_done = t_c2
            yield
        gps = PSA[:, 512 * par:512 * par + 8]
        p.waits("te", [strip_done, psfree[(0, par)]])
        t_g = None
        for g in range(NG):
            t_g = p.op("te", lambda h, g=g, gps=gps, STRIP=STRIP: h.matmul(
                gps, STRIP[:, 128 * g:128 * (g + 1)], ROUTW[:, 8 * g:8 * (g + 1)],
                start=(g == 0), stop=(g == 7)), inc="te" if g == 7 else None)
        prev_router[par] = t_g
        yield
        p.waits("ve", [t_g])
        t_m1 = p.op("ve", lambda h, gps=gps: h.reduce_max(sb(0), gps, axis=AX), inc="ve")
        yield
        p.wait("ve", t_m1)
        t_m1n = p.op("ve", lambda h: h.tensor_scalar(sb(1), sb(0), -1.0, None, ALU.mult), inc="ve")
        yield
        p.waits("sc", [t_g, t_m1n])
        t_eg = p.op("sc", lambda h, gps=gps: h.activation(SM8[:, smb:smb + 8], gps, ACT.Exp, bias=sb(1)), inc="sc")
        psfree[(0, par)] = t_eg
        yield
        p.wait("ve", t_eg)
        t_ss = p.op("ve", lambda h: h.reduce_sum(sb(2), SM8[:, smb:smb + 8], axis=AX), inc="ve")
        yield
        p.wait("sc", t_ss)
        t_ls = p.op("sc", lambda h: h.activation(sb(3), sb(2), ACT.Ln), inc="sc")
        yield
        p.wait("sc", t_ls)
        t_is = p.op("sc", lambda h: h.activation(sb(4), sb(3), ACT.Exp, scale=-1.0), inc="sc")
        yield
        p.wait("ve", t_is)
        t_pr = p.op("ve", lambda h: h.tensor_scalar(SM8[:, smb + 8:smb + 16], SM8[:, smb:smb + 8],
                                                    sb(4), None, ALU.mult), inc="ve")
        yield
        p.wait("ve", t_pr)
        t_mm = p.op("ve", lambda h: h.reduce_max(sb(5), SM8[:, smb + 8:smb + 16], axis=AX), inc="ve")
        yield
        p.wait("ve", t_mm)
        t_eq = p.op("ve", lambda h: h.tensor_scalar(SM8[:, smb + 16:smb + 24], SM8[:, smb + 8:smb + 16],
                                                    sb(5), None, ALU.is_equal), inc="ve")
        yield
        p.wait("ve", t_eq)
        t_mr = p.op("ve", lambda h: h.scalar_tensor_tensor(SM8[:, smb + 16:smb + 24], SM8[:, smb + 16:smb + 24],
                                                           NEG, SM8[:, smb + 8:smb + 16], ALU.mult, ALU.add), inc="ve")
        yield
        p.wait("ve", t_mr)
        t_m2 = p.op("ve", lambda h: h.reduce_max(sb(5), SM8[:, smb + 16:smb + 24], axis=AX), inc="ve")
        yield
        p.wait("ve", t_m2)
        t_msk = p.op("ve", lambda h: h.tensor_scalar(SM8[:, smb + 24:smb + 32], SM8[:, smb + 8:smb + 16],
                                                     sb(5), None, ALU.is_ge), inc="ve")
        yield
        p.wait("ve", t_msk)
        t_cb = p.op("ve", lambda h, tt=tt: h.tensor_tensor(
            CMB[:, 8 * tt:8 * (tt + 1)], SM8[:, smb + 8:smb + 16], SM8[:, smb + 24:smb + 32], ALU.mult), inc="ve")
        cmb_tok[tt] = t_cb

    for tp2 in range(8):
        runz(g_p6tile(2 * tp2), g_p6tile(2 * tp2 + 1), skew=8)

    if debug:
        bv = p.bar("ve")
        p.wait("gp", bv)
        t_tap = p.op("gp", lambda h: h.dma_start(taps["tap_xfbf"], XFBF), inc=p.sem("dtg"), n=16)
        p.wait("ve", t_tap)
        p.wait("te", t_tap)
        p.wait("sy", bv)
        t_tap2 = p.op("sy", lambda h: h.dma_start(taps["tap_comb"], CMB[:, :]), inc=p.sem("dtp"), n=16)
        p.wait("ve", t_tap2)

    def w1_load(i, e, fp, dep):
        p.wait("gp", dep)
        tg = p.op("gp", lambda h, i=i, e=e, fp=fp: h.dma_start(
            W1G[i], bass.AP(dw1.tensor, (e * 32 + 2 * fp) * 128 * 1024,
                            [[1024, 128], [128 * 1024, 2], [1, 1024]])), inc=p.sem("w1g%d" % i), n=16)
        tu = p.op("gp", lambda h, i=i, e=e, fp=fp: h.dma_start(
            W1U[i], bass.AP(dw1.tensor, (e * 32 + 16 + 2 * fp) * 128 * 1024,
                            [[1024, 128], [128 * 1024, 2], [1, 1024]])), inc=p.sem("w1u%d" % i), n=16)
        return (tg, tu)

    def w2_load(i, e, dh, fq, dep):
        p.wait("gp", dep)
        return p.op("gp", lambda h, i=i, e=e, dh=dh, fq=fq: h.dma_start(
            W2W[i], bass.AP(dw2.tensor, (e * 16 + 4 * fq) * 128 * 1024 + dh * 512,
                            [[1024, 128], [128 * 1024, 4], [1, 512]])), inc=p.sem("w2s%d" % i), n=16)

    PSg = (PSA, PSC)
    PSu = (PSB, PSD)
    gate_free = [p.bar("ve"), p.bar("ve")]
    pso_free = [p.bar("ve")] * 8
    t_p5te = t_p5te_glob
    w1_dma = [None, None]
    w1_free = [t_p5te, t_p5te]
    w2_dma = [None, None]
    w2_free = [t_p5te, t_p5te]

    seq = [(e, th) for e in range(NE) for th in range(2)]
    w1_dma[0] = w1_load(0, 0, 0, w1_free[0])
    for si, (e, th) in enumerate(seq):
        ybf_tok = [None] * 16
        for fp in range(8):
            i = fp % 2
            if fp + 1 < 8:
                w1_dma[(i + 1) % 2] = w1_load((i + 1) % 2, e, fp + 1, w1_free[(i + 1) % 2])
            elif si + 1 < len(seq):
                ne, nth = seq[si + 1]
                if nth == 0:
                    w1_dma[(i + 1) % 2] = w1_load((i + 1) % 2, ne, 0, w1_free[(i + 1) % 2])
                else:
                    w1_dma[(i + 1) % 2] = w1_load((i + 1) % 2, ne, 0, w1_free[(i + 1) % 2])
            p.waits("te", list(w1_dma[i]))
            for fj in range(2):
                f = 2 * fp + fj
                pp = f % 2
                p.wait("te", gate_free[pp])
                t_gu = None
                for ck in range(NG):
                    for sh2 in range(2):
                        ro = 2048 * ck + 1024 * th + 512 * sh2
                        p.op("te", lambda h, pp=pp, fj=fj, ck=ck, sh2=sh2, i=i, ro=ro: h.matmul(
                            PSg[pp][:, 512 * sh2:512 * (sh2 + 1)],
                            W1G[i][:, 1024 * fj + 128 * ck:1024 * fj + 128 * (ck + 1)],
                            XFBF[:, ro:ro + 512], start=(ck == 0), stop=(ck == 7)))
                        t_gu = p.op("te", lambda h, pp=pp, fj=fj, ck=ck, sh2=sh2, i=i, ro=ro: h.matmul(
                            PSu[pp][:, 512 * sh2:512 * (sh2 + 1)],
                            W1U[i][:, 1024 * fj + 128 * ck:1024 * fj + 128 * (ck + 1)],
                            XFBF[:, ro:ro + 512], start=(ck == 0), stop=(ck == 7)),
                            inc="te" if (ck == 7 and sh2 == 1) else None)
                p.wait("sc", t_gu)
                p.wait("sc", gate_free[1 - pp])
                t_si_ = p.op("sc", lambda h, pp=pp: h.activation(SCR[:, :], PSg[pp][:, :], ACT.Silu), inc="sc")
                p.waits("ve", [t_si_, t_gu])
                t_y = p.op("ve", lambda h, f=f, pp=pp: h.tensor_tensor(
                    YBF[:, 1024 * f:1024 * (f + 1)], SCR[:, :], PSu[pp][:, :], ALU.mult), inc="ve")
                ybf_tok[f] = t_y
                gate_free[pp] = t_y
                if fj == 1:
                    w1_free[i] = t_gu
        for dh in range(2):
            w2_dma[0] = w2_load(0, e, dh, 0, w2_free[0])
            t_o = [None] * 8
            for fq in range(4):
                i2 = fq % 2
                if fq + 1 < 4:
                    w2_dma[(i2 + 1) % 2] = w2_load((i2 + 1) % 2, e, dh, fq + 1, w2_free[(i2 + 1) % 2])
                p.wait("te", w2_dma[i2])
                for fj in range(4):
                    f = 4 * fq + fj
                    p.waits("te", [ybf_tok[f]])
                    if f == 0:
                        p.waits("te", [ybf_tok[14], ybf_tok[15]])
                        p.waits("te", pso_free)
                    for tl in range(8):
                        bank = PS[tl // 2]
                        t_o[tl] = p.op("te", lambda h, f=f, tl=tl, i2=i2, fj=fj, bank=bank: h.matmul(
                            bank[:, 512 * (tl % 2):512 * (tl % 2 + 1)],
                            YBF[:, 1024 * f + 128 * tl:1024 * f + 128 * (tl + 1)],
                            W2W[i2][:, 512 * fj:512 * (fj + 1)],
                            start=(f == 0), stop=(f == 15)),
                            inc="te" if f == 15 or fj == 3 else None)
                if fq == 3:
                    w2_free[i2] = t_o[7]
                else:
                    w2_free[i2] = t_o[7] if t_o[7] else p.bar("te")
            for tl in range(8):
                tt = 8 * th + tl
                bank = PS[tl // 2]
                p.waits("ve", [t_o[tl], cmb_tok[tt]])
                dst = X1[:, 1024 * tt + 512 * dh:1024 * tt + 512 * (dh + 1)]
                t_evo = p.op("ve", lambda h, dst=dst, bank=bank, tl=tl, tt=tt, e=e: h.scalar_tensor_tensor(
                    dst, bank[:, 512 * (tl % 2):512 * (tl % 2 + 1)],
                    col(CMB, 8 * tt + e), dst, ALU.mult, ALU.add), inc="ve")
                pso_free[tl] = t_evo
                x1_tok[tt] = t_evo
            gate_free[0] = pso_free[0]
            gate_free[1] = pso_free[4]
            for tl in range(8):
                p.wait("te", pso_free[tl])

    if debug:
        bv = p.bar("ve")
        p.wait("sy", bv)
        p.op("sy", lambda h: h.dma_start(taps["tap_x1"], X1[:, :]), inc=p.sem("dtp"), n=16)

    p.sem("dou")
    for tt in range(NT):
        p.wait("sy", x1_tok[tt])
        p.op("sy", lambda h, tt=tt: h.dma_start(
            dout[128 * tt:128 * (tt + 1), :], xtile(tt)), inc="dou", n=16)
    final_dou = p.cnt["dou"]
    p.stream["sy"].append(lambda h, sems, v=final_dou: h.wait_ge(sems["dou"], v))
    if debug:
        for snm in ("dtg", "dtp"):
            if snm in p.cnt:
                p.stream["sy"].append(
                    lambda h, sems, s2=snm, v=p.cnt[snm]: h.wait_ge(sems[s2], v))

    for s, v in p.cnt.items():
        assert v < 30000, (s, v)

    sems = {s2: ent(nc.semaphore("sem_" + s2)) for s2 in p.sem_names}

    with nc.Block() as blk:
        @blk.sync
        def _(h):
            for fn in p.stream["sy"]:
                fn(h, sems)

        @blk.gpsimd
        def _(h):
            for fn in p.stream["gp"]:
                fn(h, sems)

        @blk.scalar
        def _(h):
            for fn in p.stream["sc"]:
                fn(h, sems)

        @blk.vector
        def _(h):
            for fn in p.stream["ve"]:
                fn(h, sems)

        @blk.tensor
        def _(h):
            for fn in p.stream["te"]:
                fn(h, sems)

    for c in reversed(ctx):
        c.__exit__(None, None, None)
    return nc


def kernel(**inputs):
    shared = host_prep(inputs)
    x = np.ascontiguousarray(np.asarray(inputs["L0_attn_x"], dtype=np.float32))
    nc = build_nc(debug=DEBUG)
    in_maps = [dict(shared, x=np.ascontiguousarray(x[b])) for b in range(B)]
    res = run_bass_kernel_spmd(nc, in_maps, core_ids=list(range(B)))
    return np.stack([np.asarray(res.results[b]["out"]).reshape(T, C) for b in range(B)])


# version 12
# speedup vs baseline: 1.0114x; 1.0114x over previous
import numpy as np

import concourse.bass as bass
import concourse.mybir as mybir
from concourse.bass_utils import run_bass_kernel_spmd

F = mybir.dt.float32
BF = mybir.dt.bfloat16
ALU = mybir.AluOpType
ACT = mybir.ActivationFunctionType
AX = mybir.AxisListType.X

B, T, C = 8, 2048, 1024
NG = 8
NQ = 4
TQ = T // NQ
NT = 16
NE = 8
HB = 32
NEG = -1.0e38
EPS = 1.0e-5

DEBUG = False


class Prog:
    ENGS = ("te", "ve", "sc", "sy", "gp")

    def __init__(self):
        self.stream = {e: [] for e in self.ENGS}
        self.sem_names = ["te", "ve", "sc"]
        self.cnt = {s: 0 for s in self.sem_names}
        self.waited = {e: {} for e in self.ENGS}

    def sem(self, name):
        if name not in self.sem_names:
            self.sem_names.append(name)
            self.cnt[name] = 0
        return name

    def op(self, eng, fn, inc=None, n=1):
        tok = None
        if inc is not None:
            self.cnt[inc] += n
            tok = (inc, self.cnt[inc])

        def run(h, sems, fn=fn, inc=inc, n=n):
            r = fn(h)
            if inc is not None:
                r.then_inc(sems[inc], n)
        self.stream[eng].append(run)
        return tok

    def wait(self, eng, tok):
        if tok is None:
            return
        s, v = tok
        if self.waited[eng].get(s, 0) >= v:
            return
        self.waited[eng][s] = v

        def run(h, sems, s=s, v=v):
            h.wait_ge(sems[s], v)
        self.stream[eng].append(run)

    def waits(self, eng, toks):
        for t in toks:
            self.wait(eng, t)

    def bar(self, sem):
        return (sem, self.cnt[sem])


def _tile_lhsT(WT):
    K, M = WT.shape
    return np.ascontiguousarray(
        WT.reshape(K // 128, 128, M // 128, 128).transpose(2, 1, 0, 3)
        .reshape(M // 128, 128, K))


def _cols(vec):
    return np.ascontiguousarray(vec.reshape(8, 128).T)


def host_prep(inputs):
    g = lambda k: np.ascontiguousarray(np.asarray(inputs[k], dtype=np.float32))
    kvr = np.stack([_tile_lhsT(g("L0_attn_Wk").T), _tile_lhsT(g("L0_attn_Wv").T),
                    _tile_lhsT(g("L0_attn_Wr").T)], axis=1)
    wck = _tile_lhsT(g("L0_attn_Wck").T)
    wcv = np.ascontiguousarray(g("L0_attn_Wcv").T.reshape(32, 128, 1024))
    wcr = np.ascontiguousarray(g("L0_attn_Wcr").T.reshape(8, 128, 1024))
    routr = np.ascontiguousarray(g("L0_ffn_router_w").T.reshape(8, 128, 8))
    w1 = g("L0_ffn_w1")
    w1t = np.stack([_tile_lhsT(w1[e].T) for e in range(NE)])
    w2 = g("L0_ffn_w2")
    w2t = np.stack([np.ascontiguousarray(w2[e].T.reshape(16, 128, 1024))
                    for e in range(NE)])
    cols = np.concatenate([
        _cols(g("L0_attn_" + k)) for k in
        ("w", "u", "rk", "rv", "rr", "crk", "crr",
         "ln1_g", "ln1_b", "ln2_g", "ln2_b")], axis=1)
    cols = np.ascontiguousarray(np.pad(cols, ((0, 0), (0, 8))))
    return {
        "kvrw": kvr, "wckw": wck, "wcvw": wcv, "wcrw": wcr, "routw": routr,
        "w1w": w1t, "w2w": w2t, "cols": cols,
        "ident": np.eye(128, dtype=np.float32),
        "ones": np.ones((128, 512), dtype=np.float32),
    }


CW, CU, CRK, CRV, CRR, CCRK, CCRR, CG1, CB1, CG2, CB2 = range(11)


def build_nc(debug=False):
    nc = bass.Bass("TRN2", target_bir_lowering=False)
    p = Prog()

    dx = nc.dram_tensor("x", [T, C], F, kind="ExternalInput").ap()
    dout = nc.dram_tensor("out", [T, C], F, kind="ExternalOutput").ap()
    dkvr = nc.dram_tensor("kvrw", [NG, 3, 128, 1024], F, kind="ExternalInput").ap()
    dwck = nc.dram_tensor("wckw", [HB, 128, 1024], F, kind="ExternalInput").ap()
    dwcv = nc.dram_tensor("wcvw", [HB, 128, 1024], F, kind="ExternalInput").ap()
    dwcr = nc.dram_tensor("wcrw", [8, 128, 1024], F, kind="ExternalInput").ap()
    drout = nc.dram_tensor("routw", [8, 128, 8], F, kind="ExternalInput").ap()
    dw1 = nc.dram_tensor("w1w", [NE, 32, 128, 1024], F, kind="ExternalInput").ap()
    dw2 = nc.dram_tensor("w2w", [NE, 16, 128, 1024], F, kind="ExternalInput").ap()
    dcols = nc.dram_tensor("cols", [128, 96], F, kind="ExternalInput").ap()
    dident = nc.dram_tensor("ident", [128, 128], F, kind="ExternalInput").ap()
    dones = nc.dram_tensor("ones", [128, 512], F, kind="ExternalInput").ap()
    taps = {}
    if debug:
        for nm, w in (("tap_x1", NT * 1024), ("tap_mix", NG * TQ),
                      ("tap_xfbf", NG * T), ("tap_comb", 128),
                      ("tap_a", NT * 1024), ("tap_b", NG * TQ),
                      ("tap_c", NT * 1024)):
            taps[nm] = nc.dram_tensor(nm, [128, w], F, kind="ExternalOutput").ap()

    ctx = []
    def ent(x):
        ctx.append(x)
        return x.__enter__()

    TO = 256
    X1 = ent(nc.sbuf_tensor("X1", [128, NT * 1024], F))
    VBp = [ent(nc.sbuf_tensor("VB%d" % i, [128, TO], F)) for i in range(2)]
    EKBp = [ent(nc.sbuf_tensor("EKB%d" % i, [128, TO], F)) for i in range(2)]
    ABp = [ent(nc.sbuf_tensor("AB%d" % i, [128, TO], F)) for i in range(2)]
    T1p = [ent(nc.sbuf_tensor("T1_%d" % i, [128, TO], F)) for i in range(2)]
    T2p = [ent(nc.sbuf_tensor("T2_%d" % i, [128, TO], F)) for i in range(2)]
    T3p = [ent(nc.sbuf_tensor("T3_%d" % i, [128, TO], F)) for i in range(2)]
    RBp = [ent(nc.sbuf_tensor("RB%d" % i, [128, TO], F)) for i in range(2)]
    YBp = [ent(nc.sbuf_tensor("YB%d" % i, [128, TO], F)) for i in range(2)]
    PENp = [ent(nc.sbuf_tensor("PEN%d" % i, [128, TO + 4], F)) for i in range(2)]
    AABp = [ent(nc.sbuf_tensor("AAB%d" % i, [128, TO + 4], F)) for i in range(2)]
    SHB = ent(nc.sbuf_tensor("SHB", [128, NG * 132], F))
    XT0 = ent(nc.sbuf_tensor("XT0", [128, 1024], F))
    XT1 = ent(nc.sbuf_tensor("XT1", [128, 1024], F))
    XTILEp = [ent(nc.sbuf_tensor("XTILE%d" % i, [128, 1024], F)) for i in range(2)]
    SCR = ent(nc.sbuf_tensor("SCR", [128, 1024], F))
    SCR2 = ent(nc.sbuf_tensor("SCR2", [128, 1024], F))
    VCB = ent(nc.sbuf_tensor("VCB", [128, 1024], F))
    CS = ent(nc.sbuf_tensor("CS", [128, 16], F))
    COLS = ent(nc.sbuf_tensor("COLS", [128, 96], F))
    DCOL = ent(nc.sbuf_tensor("DCOL", [128, 64], F))
    IDENT = ent(nc.sbuf_tensor("IDENT", [128, 128], F))
    ROUTW = ent(nc.sbuf_tensor("ROUTW", [128, 64], F))
    CMB = ent(nc.sbuf_tensor("CMB", [128, 128], F))
    SM8 = ent(nc.sbuf_tensor("SM8", [128, 64], F))
    KSCR = ent(nc.sbuf_tensor("KSCR", [128, 256], F))

    BFA = ent(nc.sbuf_tensor("BFA", [128, 45056], BF))
    MIXA = BFA[:, 0:4096]
    MIXB = BFA[:, 4096:8192]
    MIXC = BFA[:, 8192:12288]
    KVRW = [BFA[:, 12288 + 3072 * i:12288 + 3072 * (i + 1)] for i in range(2)]
    WCRW = BFA[:, 12288 + 6144:12288 + 6144 + 8192]
    WCKW = [BFA[:, 26624 + 4096 * i:26624 + 4096 * (i + 1)] for i in range(2)]
    WCVW = [BFA[:, 34816 + 4096 * i:34816 + 4096 * (i + 1)] for i in range(2)]
    KCB = BFA[:, 43008 + 0:43008 + 0]
    KCB = BFA[:, 8192 + 0:8192 + 8192]
    XFBF = BFA[:, 0:16384]
    YBF = BFA[:, 16384:32768]
    W1G = [BFA[:, 32768 + 2048 * i:32768 + 2048 * (i + 1)] for i in range(2)]
    W1U = [BFA[:, 36864 + 2048 * i:36864 + 2048 * (i + 1)] for i in range(2)]
    W2W = [BFA[:, 40960 + 2048 * i:40960 + 2048 * (i + 1)] for i in range(2)]

    PSA = ent(nc.psum_tensor("PSA", [128, 1024], F))
    PSB = ent(nc.psum_tensor("PSB", [128, 1024], F))
    PSC = ent(nc.psum_tensor("PSC", [128, 1024], F))
    PSD = ent(nc.psum_tensor("PSD", [128, 1024], F))
    PS = (PSA, PSB, PSC, PSD)

    col = lambda t, j: t[:, j:j + 1]

    def bc0(colap, n):
        return bass.AP(colap.tensor, colap.offset, [[colap.ap[0][0], 128], [0, n]])
    EW = lambda g: col(DCOL, g)
    EU = lambda g: col(DCOL, 8 + g)
    PP0 = lambda g: col(DCOL, 16 + g)
    AAC = lambda g: col(DCOL, 24 + g)
    MCp = lambda par: col(DCOL, 32 + 6 * par)
    MNp = lambda par: col(DCOL, 33 + 6 * par)
    C0p = lambda par: col(DCOL, 34 + 6 * par)
    C1p = lambda par: col(DCOL, 35 + 6 * par)
    KMp = lambda par: col(DCOL, 36 + 6 * par)
    MC, MN, C0, C1 = MCp(0), MNp(0), C0p(0), C1p(0)
    ccol = lambda name, j: col(COLS, 8 * name + j)
    xtile = lambda tt: X1[:, 1024 * tt:1024 * (tt + 1)]

    p.sem("dcn")
    p.op("sy", lambda h: h.dma_start(COLS[:, :], dcols), inc="dcn", n=16)
    p.op("sy", lambda h: h.dma_start(IDENT[:, :], dident), inc="dcn", n=16)
    t_const = p.op("sy", lambda h: h.dma_start(
        ROUTW[:, :], bass.AP(drout.tensor, 0, [[8, 128], [128 * 8, 8], [1, 8]])), inc="dcn", n=16)
    t_cols = t_id = t_ones = t_rout = t_const
    p.wait("sc", t_const)
    t_ew = p.op("sc", lambda h: h.activation(DCOL[:, 0:8], COLS[:, 8 * CW:8 * CW + 8], ACT.Exp), inc="sc")
    t_eu = p.op("sc", lambda h: h.activation(DCOL[:, 8:16], COLS[:, 8 * CU:8 * CU + 8], ACT.Exp), inc="sc")
    t_pp0 = p.op("ve", lambda h: h.memset(DCOL[:, 16:24], NEG), inc="ve")
    t_eps = p.op("ve", lambda h: h.memset(DCOL[:, 56:57], EPS), inc="ve")
    t_aac = p.op("ve", lambda h: h.memset(DCOL[:, 24:32], 0.0), inc="ve")
    t_shb = p.op("ve", lambda h: h.memset(SHB[:, :], 0.0), inc="ve")
    p.waits("ve", [t_ew, t_eu, t_id, t_ones, t_rout])

    last_sq = [None, None]

    psfree = {
        (i, hh): None for i in range(4) for hh in range(2)
    }

    def transpose_to(slot_ti, slot_half, slot_pos, src_ap, extra=()):
        slot = PS[slot_ti][:, 512 * slot_half + 128 * slot_pos:
                           512 * slot_half + 128 * slot_pos + 128]
        p.waits("te", [psfree[(slot_ti, slot_half)]] + list(extra))
        t = p.op("te", lambda h, slot=slot: h.transpose(slot, src_ap, IDENT[:, :]), inc="te")
        return slot, t

    mix_tok = [None] * NQ
    x1_tok = [None] * NT
    xt_free = [(None, None), (None, None)]
    shb_tok = [None] * NG
    xtile_free = [None, None]
    shb_tok = [None] * NG
    xtile_free = [None, None]

    def runz(*gens, skew=0):
        gens = [g for g in gens if g is not None]
        if skew and len(gens) > 1:
            for j, g in enumerate(list(gens)):
                for _ in range((len(gens) - 1 - j) * skew):
                    try:
                        next(g)
                    except StopIteration:
                        gens.remove(g)
                        break
        while gens:
            for g in list(gens):
                try:
                    next(g)
                except StopIteration:
                    gens.remove(g)

    def g_ln(src_ap, dst_ap, par, dep_ve=(), dep_sc=(), out=None):
        b = 8 * par
        mu_ap, r_ap = col(CS, b + 5), col(CS, b + 6)
        p.waits("ve", list(dep_ve))
        p.waits("sc", list(dep_sc) + [last_sq[par]])
        t_s = p.op("ve", lambda h: h.reduce_sum(col(CS, b + 0), src_ap, axis=AX), inc="ve")
        yield
        t_q = p.op("sc", lambda h: h.activation(dst_ap, src_ap, ACT.Square,
                                                accum_out=col(CS, b + 1)), inc="sc")
        last_sq[par] = t_q
        yield
        p.waits("ve", [t_s, t_q])
        t_mu = p.op("ve", lambda h: h.tensor_scalar(mu_ap, col(CS, b + 0), 1.0 / C, None, ALU.mult), inc="ve")
        yield
        p.wait("ve", t_mu)
        t_m2 = p.op("ve", lambda h: h.tensor_tensor(col(CS, b + 2), mu_ap, mu_ap, ALU.mult), inc="ve")
        yield
        p.wait("ve", t_m2)
        t_va = p.op("ve", lambda h: h.scalar_tensor_tensor(
            col(CS, b + 3), col(CS, b + 1), 1.0 / C, col(CS, b + 2), ALU.mult, ALU.subtract), inc="ve")
        yield
        p.wait("sc", t_va)
        t_ln = p.op("sc", lambda h: h.activation(col(CS, b + 4), col(CS, b + 3), ACT.Ln, bias=col(DCOL, 56)), inc="sc")
        yield
        p.wait("sc", t_ln)
        t_r = p.op("sc", lambda h: h.activation(r_ap, col(CS, b + 4), ACT.Exp, scale=-0.5), inc="sc")
        yield
        p.wait("ve", t_r)
        t_d = p.op("ve", lambda h: h.tensor_scalar(dst_ap, src_ap, mu_ap, r_ap,
                                                   ALU.subtract, ALU.mult), inc="ve")
        if out is not None:
            out["d"] = t_d
            out["q"] = t_q

    def g_p12tile(tt):
        q, tl = tt // 4, tt % 4
        par = tt % 2
        xt = XT0 if par == 0 else XT1
        p.waits("sy", list(xt_free[par]))
        t_x = p.op("sy", lambda h, tt=tt, xt=xt: h.dma_start(
            xt[:, :], dx[128 * tt:128 * (tt + 1), :]), inc=p.sem("dx%d" % par), n=16)
        yield
        o1 = {}
        for _ in g_ln(xt[:, :], xtile(tt), par, dep_ve=[t_x, t_shb], dep_sc=[t_x], out=o1):
            yield
        t_xln, t_sq = o1["d"], o1["q"]
        xt_free[par] = (t_xln, t_sq)
        x1_tok[tt] = t_xln
        XTILE = XTILEp[par]
        o2 = {}
        for _ in g_ln(xtile(tt), XTILE[:, :], par, dep_ve=[t_xln, xtile_free[par]],
                      dep_sc=[t_xln, xtile_free[par]], out=o2):
            yield
        t_xa = o2["d"]
        last = None
        for g in range(NG):
            slot, t_tr = transpose_to(0, par, g % 4, XTILE[:, 128 * g:128 * (g + 1)],
                                      extra=[t_xa])
            xtile_free[par] = t_tr
            yield
            sh = SHB[:, 132 * g:132 * g + 132]
            dscr = SCR2[:, 128 * (4 * par + g % 4):128 * (4 * par + g % 4) + 128]
            p.waits("ve", [t_tr, shb_tok[g]])
            t_ap = p.op("ve", lambda h, sh=sh, slot=slot, g=g: h.tensor_scalar(
                sh[:, 1:129], slot, ccol(CG1, g), ccol(CB1, g), ALU.mult, ALU.add), inc="ve")
            psfree[(0, par)] = t_ap
            yield
            p.wait("ve", t_ap)
            t_db = p.op("ve", lambda h, sh=sh, dscr=dscr: h.tensor_tensor(
                dscr, sh[:, 1:129], sh[:, 0:128], ALU.subtract), inc="ve")
            yield
            p.wait("ve", t_db)
            t_mx = None
            for mi, (dstT, rc) in enumerate(((MIXA, CRK), (MIXB, CRV), (MIXC, CRR))):
                dst = dstT[:, 512 * g + 128 * tl:512 * g + 128 * (tl + 1)]
                t_mx = p.op("ve", lambda h, dst=dst, rc=rc, sh=sh, g=g, dscr=dscr: h.scalar_tensor_tensor(
                    dst, dscr, ccol(rc, g), sh[:, 0:128], ALU.mult, ALU.add), inc="ve")
                yield
            p.wait("ve", t_mx)
            t_bd = p.op("ve", lambda h, sh=sh: h.tensor_copy(sh[:, 0:1], sh[:, 128:129]), inc="ve")
            shb_tok[g] = t_bd
            yield
            last = t_bd
        if mix_tok[q] is None or last[1] > mix_tok[q][1]:
            mix_tok[q] = last

    def kvr_load(i, g, dep=None):
        p.wait("gp", dep)
        return p.op("gp", lambda h, i=i, g=g: h.dma_start(
            KVRW[i], bass.AP(dkvr.tensor, g * 3 * 128 * 1024,
                             [[1024, 128], [128 * 1024, 3], [1, 1024]])), inc=p.sem("kvr%d" % i), n=16)

    kvr_dma = [None, None]
    kvr_free = [None, None]

    def g_p3(ii, iters):
        q, oj, g = iters[ii]
        par = ii % 2
        slot = KVRW[par]
        t_w = kvr_dma[par]
        ob = TO * oj
        mix = mix_tok[q]
        MC, MN, C0, C1, KM = MCp(par), MNp(par), C0p(par), C1p(par), KMp(par)
        EKB, VB, RB, YB = EKBp[par], VBp[par], RBp[par], YBp[par]
        T1, T2, T3, AB = T1p[par], T2p[par], T3p[par], ABp[par]
        PEN, AAB = PENp[par], AABp[par]
        psl = lambda t: t[:, 512 * par:512 * par + TO]
        p.waits("te", [mix, t_w, psfree[(1, par)]])
        tk = None
        for ck in range(NG):
            tk = p.op("te", lambda h, ck=ck, slot=slot, par=par, ob=ob: h.matmul(
                PSB[:, 512 * par:512 * par + TO], slot[:, 128 * ck:128 * (ck + 1)],
                MIXA[:, 512 * ck + ob:512 * ck + ob + TO], start=(ck == 0), stop=(ck == 7)),
                inc="te" if ck == 7 else None)
        yield
        p.wait("te", psfree[(2, par)])
        tv = None
        for ck in range(NG):
            tv = p.op("te", lambda h, ck=ck, slot=slot, par=par, ob=ob: h.matmul(
                PSC[:, 512 * par:512 * par + TO], slot[:, 1024 + 128 * ck:1024 + 128 * (ck + 1)],
                MIXB[:, 512 * ck + ob:512 * ck + ob + TO], start=(ck == 0), stop=(ck == 7)),
                inc="te" if ck == 7 else None)
        yield
        p.wait("te", psfree[(3, par)])
        tr = None
        for ck in range(NG):
            tr = p.op("te", lambda h, ck=ck, slot=slot, par=par, ob=ob: h.matmul(
                PSD[:, 512 * par:512 * par + TO], slot[:, 2048 + 128 * ck:2048 + 128 * (ck + 1)],
                MIXC[:, 512 * ck + ob:512 * ck + ob + TO], start=(ck == 0), stop=(ck == 7)),
                inc="te" if ck == 7 else None)
        kvr_free[par] = tr
        if ii + 2 < len(iters):
            kvr_dma[par] = kvr_load(par, iters[ii + 2][2], dep=tr)
        yield
        p.wait("ve", tk)
        t_km = p.op("ve", lambda h: h.reduce_max(KM, psl(PSB), axis=AX), inc="ve")
        yield
        p.wait("ve", t_km)
        t_M = p.op("ve", lambda h, g=g: h.tensor_tensor(MC, KM, PP0(g), ALU.max), inc="ve")
        yield
        p.wait("ve", t_M)
        t_MN = p.op("ve", lambda h: h.tensor_scalar(MN, MC, -1.0, None, ALU.mult), inc="ve")
        yield
        p.waits("sc", [tk, t_MN])
        t_ek = p.op("sc", lambda h: h.activation(EKB[:, :], psl(PSB), ACT.Exp, bias=MN), inc="sc")
        psfree[(1, par)] = t_ek
        yield
        p.wait("sc", tv)
        t_vev = p.op("sc", lambda h: h.copy(VB[:, :], psl(PSC)), inc="sc")
        psfree[(2, par)] = t_vev
        yield
        p.wait("sc", tr)
        t_r1 = p.op("sc", lambda h: h.activation(RB[:, :], psl(PSD), ACT.Exp, scale=-1.0), inc="sc")
        psfree[(3, par)] = t_r1
        yield
        p.wait("ve", t_r1)
        t_r2 = p.op("ve", lambda h: h.tensor_scalar(T3[:, :], RB[:, :], 1.0, None, ALU.add), inc="ve")
        yield
        p.wait("sc", t_r2)
        t_r3 = p.op("sc", lambda h: h.activation(T2[:, :], T3[:, :], ACT.Ln), inc="sc")
        yield
        p.wait("sc", t_r3)
        t_r4 = p.op("sc", lambda h: h.activation(RB[:, :], T2[:, :], ACT.Exp, scale=-1.0), inc="sc")
        yield
        p.waits("ve", [t_ek, t_ew, t_eu])
        t_d1 = p.op("ve", lambda h, g=g: h.tensor_scalar(T1[:, :], EKB[:, :], EW(g), None, ALU.mult), inc="ve")
        yield
        p.wait("sc", t_d1)
        t_pi = p.op("sc", lambda h, g=g: h.activation(PEN[:, 0:1], PP0(g), ACT.Exp, bias=MN), inc="sc")
        yield
        p.waits("ve", [t_d1, t_pi])
        t_ps = p.op("ve", lambda h, g=g: h.tensor_tensor_scan(
            PEN[:, 1:TO + 1], bc0(EW(g), TO), T1[:, :], PEN[:, 0:1], ALU.mult, ALU.add), inc="ve")
        yield
        p.wait("sc", t_ps)
        t_lp = p.op("sc", lambda h: h.activation(C1, PEN[:, TO:TO + 1], ACT.Ln), inc="sc")
        yield
        p.wait("ve", t_lp)
        t_pp = p.op("ve", lambda h, g=g: h.tensor_tensor(PP0(g), C1, MC, ALU.add), inc="ve")
        yield
        t_em = p.op("ve", lambda h: h.tensor_tensor(T2[:, :], PEN[:, 0:TO], EKB[:, :], ALU.max), inc="ve")
        yield
        p.wait("sc", t_em)
        t_l1 = p.op("sc", lambda h: h.activation(T3[:, :], T2[:, :], ACT.Ln), inc="sc")
        yield
        p.wait("sc", t_l1)
        t_iv = p.op("sc", lambda h: h.activation(T2[:, :], T3[:, :], ACT.Exp, scale=-1.0), inc="sc")
        yield
        p.wait("ve", t_iv)
        t_al = p.op("ve", lambda h: h.tensor_tensor(AB[:, :], PEN[:, 0:TO], T2[:, :], ALU.mult), inc="ve")
        yield
        t_be = p.op("ve", lambda h: h.tensor_tensor(T1[:, :], EKB[:, :], T2[:, :], ALU.mult), inc="ve")
        yield
        p.waits("ve", [t_be, t_vev])
        t_bv = p.op("ve", lambda h: h.tensor_tensor(T3[:, :], T1[:, :], VB[:, :], ALU.mult), inc="ve")
        yield
        t_a0 = p.op("ve", lambda h, g=g: h.tensor_copy(AAB[:, 0:1], AAC(g)), inc="ve")
        yield
        p.waits("ve", [t_al, t_bv, t_a0])
        t_as = p.op("ve", lambda h: h.tensor_tensor_scan(
            AAB[:, 1:TO + 1], AB[:, :], T3[:, :], AAB[:, 0:1], ALU.mult, ALU.add), inc="ve")
        yield
        p.wait("ve", t_as)
        t_ac = p.op("ve", lambda h, g=g: h.tensor_copy(AAC(g), AAB[:, TO:TO + 1]), inc="ve")
        yield
        t_Et = p.op("ve", lambda h, g=g: h.tensor_scalar(T1[:, :], EKB[:, :], EU(g), None, ALU.mult), inc="ve")
        yield
        p.wait("ve", t_Et)
        t_n1 = p.op("ve", lambda h: h.tensor_tensor(T2[:, :], PEN[:, 0:TO], AAB[:, 0:TO], ALU.mult), inc="ve")
        yield
        t_n2 = p.op("ve", lambda h: h.tensor_tensor(EKB[:, :], T1[:, :], VB[:, :], ALU.mult), inc="ve")
        yield
        p.waits("ve", [t_n1, t_n2])
        t_nu = p.op("ve", lambda h: h.tensor_tensor(T2[:, :], T2[:, :], EKB[:, :], ALU.add), inc="ve")
        yield
        t_de = p.op("ve", lambda h: h.tensor_tensor(T1[:, :], PEN[:, 0:TO], T1[:, :], ALU.add), inc="ve")
        yield
        p.waits("sc", [t_de])
        t_l2 = p.op("sc", lambda h: h.activation(T3[:, :], T1[:, :], ACT.Ln), inc="sc")
        yield
        p.wait("sc", t_l2)
        t_rd = p.op("sc", lambda h: h.activation(T1[:, :], T3[:, :], ACT.Exp, scale=-1.0), inc="sc")
        yield
        p.waits("ve", [t_nu, t_rd, t_r4])
        t_y = p.op("ve", lambda h: h.tensor_tensor(YB[:, :], T2[:, :], T1[:, :], ALU.mult), inc="ve")
        yield
        p.wait("ve", t_y)
        t_sy = p.op("ve", lambda h: h.tensor_tensor(YB[:, :], YB[:, :], RB[:, :], ALU.mult), inc="ve")
        yield
        for b in range(2):
            tt = 2 * (2 * q + oj) + b
            slot2, t_tr = transpose_to(0, par, g % 4, YB[:, 128 * b:128 * (b + 1)],
                                       extra=[t_sy])
            yield
            dst = X1[:, 1024 * tt + 128 * g:1024 * tt + 128 * (g + 1)]
            p.waits("ve", [t_tr, x1_tok[tt]])
            t_x1 = p.op("ve", lambda h, dst=dst, slot2=slot2: h.tensor_tensor(
                dst, dst, slot2, ALU.add), inc="ve")
            psfree[(0, par)] = t_x1
            x1_tok[tt] = t_x1
            yield

    iters = [(q, oj, g) for q in range(NQ) for oj in range(2) for g in range(NG)]
    kvr_dma[0] = kvr_load(0, iters[0][2])
    kvr_dma[1] = kvr_load(1, iters[1][2])
    for q in range(NQ):
        runz(g_p12tile(4 * q + 0), g_p12tile(4 * q + 1), skew=10)
        runz(g_p12tile(4 * q + 2), g_p12tile(4 * q + 3), skew=10)
        base = q * 16
        for j in range(8):
            runz(g_p3(base + 2 * j, iters), g_p3(base + 2 * j + 1, iters), skew=4)

    if debug:
        bv = p.bar("ve")
        p.wait("sy", bv)
        t_tap = p.op("sy", lambda h: h.dma_start(taps["tap_a"], X1[:, :]), inc=p.sem("dtp"), n=16)
        p.wait("ve", t_tap)
        p.wait("gp", bv)
        t_tap = p.op("gp", lambda h: h.dma_start(taps["tap_b"], MIXA[:, :]), inc=p.sem("dtg"), n=16)
        p.wait("ve", t_tap)
        p.wait("te", t_tap)

    p.wait("ve", p.bar("ve"))
    t_shb2 = p.op("ve", lambda h: h.memset(SHB[:, :], 0.0), inc="ve")
    mix2_tok = [None] * NQ

    def g_p4tile(tt):
        q, tl = tt // 4, tt % 4
        par = tt % 2
        xt = XT0 if par == 0 else XT1
        p.waits("sy", list(xt_free[par]))
        t_x = p.op("sy", lambda h, tt=tt, xt=xt: h.dma_start(
            xt[:, :], dx[128 * tt:128 * (tt + 1), :]), inc=p.sem("dx%d" % par), n=16)
        yield
        XTILE = XTILEp[par]
        o = {}
        for _ in g_ln(xtile(tt), XTILE[:, :], par,
                      dep_ve=[t_shb2, x1_tok[tt], xtile_free[par]],
                      dep_sc=[x1_tok[tt], xtile_free[par]], out=o):
            yield
        t_xb, t_sq = o["d"], o["q"]
        p.waits("ve", [t_xb, t_x])
        t_x2a = p.op("ve", lambda h, tt=tt, xt=xt: h.tensor_tensor(
            xtile(tt), xtile(tt), xt[:, :], ALU.add), inc="ve")
        x1_tok[tt] = t_x2a
        xt_free[par] = (t_x2a, t_sq)
        yield
        last = None
        for g in range(NG):
            slot, t_tr = transpose_to(0, par, g % 4, XTILE[:, 128 * g:128 * (g + 1)],
                                      extra=[t_xb])
            xtile_free[par] = t_tr
            yield
            sh = SHB[:, 132 * g:132 * g + 132]
            dscr = SCR2[:, 128 * (4 * par + g % 4):128 * (4 * par + g % 4) + 128]
            p.waits("ve", [t_tr, shb_tok[g]])
            t_ap = p.op("ve", lambda h, sh=sh, slot=slot, g=g: h.tensor_scalar(
                sh[:, 1:129], slot, ccol(CG2, g), ccol(CB2, g), ALU.mult, ALU.add), inc="ve")
            psfree[(0, par)] = t_ap
            yield
            p.wait("ve", t_ap)
            t_db = p.op("ve", lambda h, sh=sh, dscr=dscr: h.tensor_tensor(
                dscr, sh[:, 1:129], sh[:, 0:128], ALU.subtract), inc="ve")
            yield
            p.wait("ve", t_db)
            t_mx = None
            for dstT, rc in ((MIXA, CCRK), (MIXB, CCRR)):
                dst = dstT[:, 512 * g + 128 * tl:512 * g + 128 * (tl + 1)]
                t_mx = p.op("ve", lambda h, dst=dst, rc=rc, sh=sh, g=g, dscr=dscr: h.scalar_tensor_tensor(
                    dst, dscr, ccol(rc, g), sh[:, 0:128], ALU.mult, ALU.add), inc="ve")
                yield
            p.wait("ve", t_mx)
            t_bd = p.op("ve", lambda h, sh=sh: h.tensor_copy(sh[:, 0:1], sh[:, 128:129]), inc="ve")
            shb_tok[g] = t_bd
            yield
            last = t_bd
        if mix2_tok[q] is None or last[1] > mix2_tok[q][1]:
            mix2_tok[q] = last

    def emit_p4(q):
        runz(g_p4tile(4 * q + 0), g_p4tile(4 * q + 1), skew=10)
        runz(g_p4tile(4 * q + 2), g_p4tile(4 * q + 3), skew=10)

    t_wcr = p.op("gp", lambda h: h.dma_start(
        WCRW, bass.AP(dwcr.tensor, 0, [[1024, 128], [128 * 1024, 8], [1, 1024]])), inc=p.sem("wcr"), n=16)

    def wseq_load(dst_ring, i, dram, hb4, dep=None, tag=""):
        p.wait("gp", dep)
        return p.op("gp", lambda h, i=i, hb4=hb4: h.dma_start(
            dst_ring[i], bass.AP(dram.tensor, hb4 * 4 * 128 * 1024,
                                 [[1024, 128], [128 * 1024, 4], [1, 1024]])),
            inc=p.sem("%s%d" % (tag, i)), n=16)

    wck_dma = [None, None]
    wcv_dma = [None, None]
    wck_free = [None, None]
    wcv_free = [None, None]
    kc_tok = [None] * HB
    vcb_free = None

    def emit_p5(oct_):
        nonlocal vcb_free
        q, oj = oct_ // 2, oct_ % 2
        mix = mix2_tok[q]
        ob = 256 * oj
        wck_dma[0] = wseq_load(WCKW, 0, dwck, 0, dep=wck_free[0], tag='wck')
        for hb4 in range(8):
            i = hb4 % 2
            if hb4 + 1 < 8:
                wck_dma[(i + 1) % 2] = wseq_load(WCKW, (i + 1) % 2, dwck, hb4 + 1,
                                                 dep=wck_free[(i + 1) % 2], tag='wck')
            p.waits("te", [mix, wck_dma[i]])
            for hj in range(4):
                hblk = 4 * hb4 + hj
                half = hblk % 2
                p.wait("te", psfree[(2, half)])
                t_mm = None
                for ck in range(NG):
                    t_mm = p.op("te", lambda h, hj=hj, ck=ck, half=half, i=i, ob=ob: h.matmul(
                        PSC[:, 512 * half + 0:512 * half + 256],
                        WCKW[i][:, 1024 * hj + 128 * ck:1024 * hj + 128 * (ck + 1)],
                        MIXA[:, 512 * ck + ob:512 * ck + ob + 256],
                        start=(ck == 0), stop=(ck == 7)), inc="te" if ck == 7 else None)
                p.wait("sc", t_mm)
                t_rl = p.op("sc", lambda h, half=half: h.activation(
                    KSCR[:, :], PSC[:, 512 * half:512 * half + 256], ACT.Relu), inc="sc")
                p.waits("ve", [t_rl, t_mm])
                t_kc = p.op("ve", lambda h, hblk=hblk, half=half: h.tensor_tensor(
                    KCB[:, 256 * hblk:256 * (hblk + 1)], KSCR[:, :],
                    PSC[:, 512 * half:512 * half + 256], ALU.mult), inc="ve")
                p.wait("sc", t_kc)
                kc_tok[hblk] = t_kc
                psfree[(2, half)] = t_kc
                if hj == 3:
                    wck_free[i] = t_mm
        wcv_dma[0] = wseq_load(WCVW, 0, dwcv, 0, dep=wcv_free[0], tag='wcv')
        t_vc = [None, None]
        for hb4 in range(8):
            i = hb4 % 2
            if hb4 + 1 < 8:
                wcv_dma[(i + 1) % 2] = wseq_load(WCVW, (i + 1) % 2, dwcv, hb4 + 1,
                                                 dep=wcv_free[(i + 1) % 2], tag='wcv')
            p.wait("te", wcv_dma[i])
            for hj in range(4):
                hblk = 4 * hb4 + hj
                p.wait("te", kc_tok[hblk])
                if hblk == 0:
                    p.waits("te", [psfree[(0, 0)], psfree[(0, 1)],
                                   psfree[(1, 0)], psfree[(1, 1)]])
                t_last = None
                for t2 in range(2):
                    for dh in range(2):
                        t_last = p.op("te", lambda h, hblk=hblk, t2=t2, dh=dh, i=i, hj=hj: h.matmul(
                            PS[t2][:, 512 * dh:512 * (dh + 1)],
                            KCB[:, 256 * hblk + 128 * t2:256 * hblk + 128 * (t2 + 1)],
                            WCVW[i][:, 1024 * hj + 512 * dh:1024 * hj + 512 * (dh + 1)],
                            start=(hblk == 0), stop=(hblk == HB - 1)),
                            inc="te" if (hblk == HB - 1 and t2 == 1 and dh == 1) or (hj == 3 and t2 == 1 and dh == 1) else None)
                t_vc[1] = t_last
            if hb4 == 7:
                wcv_free[i] = t_last
            else:
                wcv_free[i] = t_last
        t_vc_done = t_vc[1]
        for t2 in range(2):
            tt = 2 * oct_ + t2
            p.waits("ve", [t_vc_done, vcb_free])
            t_ev = p.op("ve", lambda h, t2=t2: h.tensor_copy(VCB[:, :], PS[t2][:, :]), inc="ve")
            psfree[(t2, 0)] = t_ev
            psfree[(t2, 1)] = t_ev
            p.waits("te", [mix, t_wcr, psfree[(3, 0)], psfree[(3, 1)]])
            t_cr = None
            for ck in range(NG):
                for dh in range(2):
                    t_cr = p.op("te", lambda h, ck=ck, dh=dh, t2=t2, ob=ob: h.matmul(
                        PSD[:, 512 * dh:512 * (dh + 1)],
                        MIXB[:, 512 * ck + ob + 128 * t2:512 * ck + ob + 128 * (t2 + 1)],
                        WCRW[:, 1024 * ck + 512 * dh:1024 * ck + 512 * (dh + 1)],
                        start=(ck == 0), stop=(ck == 7)),
                        inc="te" if (ck == 7 and dh == 1) else None)
            p.wait("sc", t_cr)
            t_s1 = p.op("sc", lambda h: h.activation(SCR[:, :], PSD[:, :], ACT.Exp, scale=-1.0), inc="sc")
            psfree[(3, 0)] = t_s1
            psfree[(3, 1)] = t_s1
            p.wait("ve", t_s1)
            t_s2 = p.op("ve", lambda h: h.tensor_scalar(SCR2[:, :], SCR[:, :], 1.0, None, ALU.add), inc="ve")
            p.wait("sc", t_s2)
            t_s3 = p.op("sc", lambda h: h.activation(SCR[:, :], SCR2[:, :], ACT.Ln), inc="sc")
            p.wait("sc", t_s3)
            t_s4 = p.op("sc", lambda h: h.activation(SCR2[:, :], SCR[:, :], ACT.Exp, scale=-1.0), inc="sc")
            p.wait("sc", t_s4)
            p.waits("ve", [t_s4, t_ev])
            t_m = p.op("ve", lambda h: h.tensor_tensor(VCB[:, :], SCR2[:, :], VCB[:, :], ALU.mult), inc="ve")
            p.waits("ve", [t_m, x1_tok[tt]])
            t_x2 = p.op("ve", lambda h, tt=tt: h.tensor_tensor(
                xtile(tt), xtile(tt), VCB[:, :], ALU.add), inc="ve")
            x1_tok[tt] = t_x2
            vcb_free = t_x2


    for q in range(NQ):
        emit_p4(q)
        if debug and q == NQ - 1:
            bv = p.bar("ve")
            p.wait("gp", bv)
            t_tapm = p.op("gp", lambda h: h.dma_start(taps["tap_mix"], MIXA[:, :]), inc=p.sem("dtg"), n=16)
            p.wait("ve", t_tapm)
            p.wait("te", t_tapm)
        emit_p5(2 * q)
        emit_p5(2 * q + 1)
    t_p5te_glob = p.bar("te")

    if debug:
        bv = p.bar("ve")
        p.wait("sy", bv)
        t_tap = p.op("sy", lambda h: h.dma_start(taps["tap_c"], X1[:, :]), inc=p.sem("dtp"), n=16)
        p.wait("ve", t_tap)

    tr6 = {0: p.bar("ve"), 1: p.bar("ve")}
    prev_router = {0: None, 1: None}
    strip_bufs = (SCR2, VCB)
    cmb_tok = [None] * NT

    def g_p6tile(tt):
        par = tt % 2
        sb = lambda j: col(DCOL, 44 + 6 * par + j)
        smb = 32 * par
        STRIP = strip_bufs[par]
        XTILE = XTILEp[par]
        o = {}
        for _ in g_ln(xtile(tt), XTILE[:, :], par,
                      dep_ve=[x1_tok[tt], xtile_free[par]],
                      dep_sc=[x1_tok[tt], xtile_free[par]], out=o):
            yield
        t_xf = o["d"]
        strip_done = None
        for g in range(NG):
            ti = 1 + g // 4
            slot, t_tr = transpose_to(ti, par, g % 4, XTILE[:, 128 * g:128 * (g + 1)],
                                      extra=[t_xf, prev_router[par]])
            xtile_free[par] = t_tr
            yield
            p.waits("ve", [t_tr, prev_router[par]])
            t_c1 = p.op("ve", lambda h, g=g, slot=slot, STRIP=STRIP: h.tensor_copy(
                STRIP[:, 128 * g:128 * (g + 1)], slot), inc="ve")
            psfree[(ti, par)] = t_c1
            yield
            p.wait("ve", t_c1)
            t_c2 = p.op("ve", lambda h, g=g, tt=tt, STRIP=STRIP: h.tensor_copy(
                XFBF[:, 2048 * g + 128 * tt:2048 * g + 128 * (tt + 1)],
                STRIP[:, 128 * g:128 * (g + 1)]), inc="ve")
            strip_done = t_c2
            yield
        gps = PSA[:, 512 * par:512 * par + 8]
        p.waits("te", [strip_done, psfree[(0, par)]])
        t_g = None
        for g in range(NG):
            t_g = p.op("te", lambda h, g=g, gps=gps, STRIP=STRIP: h.matmul(
                gps, STRIP[:, 128 * g:128 * (g + 1)], ROUTW[:, 8 * g:8 * (g + 1)],
                start=(g == 0), stop=(g == 7)), inc="te" if g == 7 else None)
        prev_router[par] = t_g
        yield
        p.waits("ve", [t_g])
        t_m1 = p.op("ve", lambda h, gps=gps: h.reduce_max(sb(0), gps, axis=AX), inc="ve")
        yield
        p.wait("ve", t_m1)
        t_m1n = p.op("ve", lambda h: h.tensor_scalar(sb(1), sb(0), -1.0, None, ALU.mult), inc="ve")
        yield
        p.waits("sc", [t_g, t_m1n])
        t_eg = p.op("sc", lambda h, gps=gps: h.activation(SM8[:, smb:smb + 8], gps, ACT.Exp, bias=sb(1)), inc="sc")
        psfree[(0, par)] = t_eg
        yield
        p.wait("ve", t_eg)
        t_ss = p.op("ve", lambda h: h.reduce_sum(sb(2), SM8[:, smb:smb + 8], axis=AX), inc="ve")
        yield
        p.wait("sc", t_ss)
        t_ls = p.op("sc", lambda h: h.activation(sb(3), sb(2), ACT.Ln), inc="sc")
        yield
        p.wait("sc", t_ls)
        t_is = p.op("sc", lambda h: h.activation(sb(4), sb(3), ACT.Exp, scale=-1.0), inc="sc")
        yield
        p.wait("ve", t_is)
        t_pr = p.op("ve", lambda h: h.tensor_scalar(SM8[:, smb + 8:smb + 16], SM8[:, smb:smb + 8],
                                                    sb(4), None, ALU.mult), inc="ve")
        yield
        p.wait("ve", t_pr)
        t_mm = p.op("ve", lambda h: h.reduce_max(sb(5), SM8[:, smb + 8:smb + 16], axis=AX), inc="ve")
        yield
        p.wait("ve", t_mm)
        t_eq = p.op("ve", lambda h: h.tensor_scalar(SM8[:, smb + 16:smb + 24], SM8[:, smb + 8:smb + 16],
                                                    sb(5), None, ALU.is_equal), inc="ve")
        yield
        p.wait("ve", t_eq)
        t_mr = p.op("ve", lambda h: h.scalar_tensor_tensor(SM8[:, smb + 16:smb + 24], SM8[:, smb + 16:smb + 24],
                                                           NEG, SM8[:, smb + 8:smb + 16], ALU.mult, ALU.add), inc="ve")
        yield
        p.wait("ve", t_mr)
        t_m2 = p.op("ve", lambda h: h.reduce_max(sb(5), SM8[:, smb + 16:smb + 24], axis=AX), inc="ve")
        yield
        p.wait("ve", t_m2)
        t_msk = p.op("ve", lambda h: h.tensor_scalar(SM8[:, smb + 24:smb + 32], SM8[:, smb + 8:smb + 16],
                                                     sb(5), None, ALU.is_ge), inc="ve")
        yield
        p.wait("ve", t_msk)
        t_cb = p.op("ve", lambda h, tt=tt: h.tensor_tensor(
            CMB[:, 8 * tt:8 * (tt + 1)], SM8[:, smb + 8:smb + 16], SM8[:, smb + 24:smb + 32], ALU.mult), inc="ve")
        cmb_tok[tt] = t_cb

    for tp2 in range(8):
        runz(g_p6tile(2 * tp2), g_p6tile(2 * tp2 + 1))

    if debug:
        bv = p.bar("ve")
        p.wait("gp", bv)
        t_tap = p.op("gp", lambda h: h.dma_start(taps["tap_xfbf"], XFBF), inc=p.sem("dtg"), n=16)
        p.wait("ve", t_tap)
        p.wait("te", t_tap)
        p.wait("sy", bv)
        t_tap2 = p.op("sy", lambda h: h.dma_start(taps["tap_comb"], CMB[:, :]), inc=p.sem("dtp"), n=16)
        p.wait("ve", t_tap2)

    def w1_load(i, e, fp, dep):
        p.wait("gp", dep)
        tg = p.op("gp", lambda h, i=i, e=e, fp=fp: h.dma_start(
            W1G[i], bass.AP(dw1.tensor, (e * 32 + 2 * fp) * 128 * 1024,
                            [[1024, 128], [128 * 1024, 2], [1, 1024]])), inc=p.sem("w1g%d" % i), n=16)
        tu = p.op("gp", lambda h, i=i, e=e, fp=fp: h.dma_start(
            W1U[i], bass.AP(dw1.tensor, (e * 32 + 16 + 2 * fp) * 128 * 1024,
                            [[1024, 128], [128 * 1024, 2], [1, 1024]])), inc=p.sem("w1u%d" % i), n=16)
        return (tg, tu)

    def w2_load(i, e, dh, fq, dep):
        p.wait("gp", dep)
        return p.op("gp", lambda h, i=i, e=e, dh=dh, fq=fq: h.dma_start(
            W2W[i], bass.AP(dw2.tensor, (e * 16 + 4 * fq) * 128 * 1024 + dh * 512,
                            [[1024, 128], [128 * 1024, 4], [1, 512]])), inc=p.sem("w2s%d" % i), n=16)

    PSg = (PSA, PSC)
    PSu = (PSB, PSD)
    gate_free = [p.bar("ve"), p.bar("ve")]
    pso_free = [p.bar("ve")] * 8
    t_p5te = t_p5te_glob
    w1_dma = [None, None]
    w1_free = [t_p5te, t_p5te]
    w2_dma = [None, None]
    w2_free = [t_p5te, t_p5te]

    seq = [(e, th) for e in range(NE) for th in range(2)]
    w1_dma[0] = w1_load(0, 0, 0, w1_free[0])
    for si, (e, th) in enumerate(seq):
        ybf_tok = [None] * 16
        for fp in range(8):
            i = fp % 2
            if fp + 1 < 8:
                w1_dma[(i + 1) % 2] = w1_load((i + 1) % 2, e, fp + 1, w1_free[(i + 1) % 2])
            elif si + 1 < len(seq):
                ne, nth = seq[si + 1]
                if nth == 0:
                    w1_dma[(i + 1) % 2] = w1_load((i + 1) % 2, ne, 0, w1_free[(i + 1) % 2])
                else:
                    w1_dma[(i + 1) % 2] = w1_load((i + 1) % 2, ne, 0, w1_free[(i + 1) % 2])
            p.waits("te", list(w1_dma[i]))
            for fj in range(2):
                f = 2 * fp + fj
                pp = f % 2
                p.wait("te", gate_free[pp])
                t_gu = None
                for ck in range(NG):
                    for sh2 in range(2):
                        ro = 2048 * ck + 1024 * th + 512 * sh2
                        p.op("te", lambda h, pp=pp, fj=fj, ck=ck, sh2=sh2, i=i, ro=ro: h.matmul(
                            PSg[pp][:, 512 * sh2:512 * (sh2 + 1)],
                            W1G[i][:, 1024 * fj + 128 * ck:1024 * fj + 128 * (ck + 1)],
                            XFBF[:, ro:ro + 512], start=(ck == 0), stop=(ck == 7)))
                        t_gu = p.op("te", lambda h, pp=pp, fj=fj, ck=ck, sh2=sh2, i=i, ro=ro: h.matmul(
                            PSu[pp][:, 512 * sh2:512 * (sh2 + 1)],
                            W1U[i][:, 1024 * fj + 128 * ck:1024 * fj + 128 * (ck + 1)],
                            XFBF[:, ro:ro + 512], start=(ck == 0), stop=(ck == 7)),
                            inc="te" if (ck == 7 and sh2 == 1) else None)
                p.wait("sc", t_gu)
                p.wait("sc", gate_free[1 - pp])
                t_si_ = p.op("sc", lambda h, pp=pp: h.activation(SCR[:, :], PSg[pp][:, :], ACT.Silu), inc="sc")
                p.waits("ve", [t_si_, t_gu])
                t_y = p.op("ve", lambda h, f=f, pp=pp: h.tensor_tensor(
                    YBF[:, 1024 * f:1024 * (f + 1)], SCR[:, :], PSu[pp][:, :], ALU.mult), inc="ve")
                ybf_tok[f] = t_y
                gate_free[pp] = t_y
                if fj == 1:
                    w1_free[i] = t_gu
        for dh in range(2):
            w2_dma[0] = w2_load(0, e, dh, 0, w2_free[0])
            t_o = [None] * 8
            for fq in range(4):
                i2 = fq % 2
                if fq + 1 < 4:
                    w2_dma[(i2 + 1) % 2] = w2_load((i2 + 1) % 2, e, dh, fq + 1, w2_free[(i2 + 1) % 2])
                p.wait("te", w2_dma[i2])
                for fj in range(4):
                    f = 4 * fq + fj
                    p.waits("te", [ybf_tok[f]])
                    if f == 0:
                        p.waits("te", [ybf_tok[14], ybf_tok[15]])
                        p.waits("te", pso_free)
                    for tl in range(8):
                        bank = PS[tl // 2]
                        t_o[tl] = p.op("te", lambda h, f=f, tl=tl, i2=i2, fj=fj, bank=bank: h.matmul(
                            bank[:, 512 * (tl % 2):512 * (tl % 2 + 1)],
                            YBF[:, 1024 * f + 128 * tl:1024 * f + 128 * (tl + 1)],
                            W2W[i2][:, 512 * fj:512 * (fj + 1)],
                            start=(f == 0), stop=(f == 15)),
                            inc="te" if f == 15 or fj == 3 else None)
                if fq == 3:
                    w2_free[i2] = t_o[7]
                else:
                    w2_free[i2] = t_o[7] if t_o[7] else p.bar("te")
            for tl in range(8):
                tt = 8 * th + tl
                bank = PS[tl // 2]
                p.waits("ve", [t_o[tl], cmb_tok[tt]])
                dst = X1[:, 1024 * tt + 512 * dh:1024 * tt + 512 * (dh + 1)]
                t_evo = p.op("ve", lambda h, dst=dst, bank=bank, tl=tl, tt=tt, e=e: h.scalar_tensor_tensor(
                    dst, bank[:, 512 * (tl % 2):512 * (tl % 2 + 1)],
                    col(CMB, 8 * tt + e), dst, ALU.mult, ALU.add), inc="ve")
                pso_free[tl] = t_evo
                x1_tok[tt] = t_evo
            gate_free[0] = pso_free[0]
            gate_free[1] = pso_free[4]
            for tl in range(8):
                p.wait("te", pso_free[tl])

    if debug:
        bv = p.bar("ve")
        p.wait("sy", bv)
        p.op("sy", lambda h: h.dma_start(taps["tap_x1"], X1[:, :]), inc=p.sem("dtp"), n=16)

    p.sem("dou")
    for tt in range(NT):
        p.wait("sy", x1_tok[tt])
        p.op("sy", lambda h, tt=tt: h.dma_start(
            dout[128 * tt:128 * (tt + 1), :], xtile(tt)), inc="dou", n=16)
    final_dou = p.cnt["dou"]
    p.stream["sy"].append(lambda h, sems, v=final_dou: h.wait_ge(sems["dou"], v))
    if debug:
        for snm in ("dtg", "dtp"):
            if snm in p.cnt:
                p.stream["sy"].append(
                    lambda h, sems, s2=snm, v=p.cnt[snm]: h.wait_ge(sems[s2], v))

    for s, v in p.cnt.items():
        assert v < 30000, (s, v)

    sems = {s2: ent(nc.semaphore("sem_" + s2)) for s2 in p.sem_names}

    with nc.Block() as blk:
        @blk.sync
        def _(h):
            for fn in p.stream["sy"]:
                fn(h, sems)

        @blk.gpsimd
        def _(h):
            for fn in p.stream["gp"]:
                fn(h, sems)

        @blk.scalar
        def _(h):
            for fn in p.stream["sc"]:
                fn(h, sems)

        @blk.vector
        def _(h):
            for fn in p.stream["ve"]:
                fn(h, sems)

        @blk.tensor
        def _(h):
            for fn in p.stream["te"]:
                fn(h, sems)

    for c in reversed(ctx):
        c.__exit__(None, None, None)
    return nc


def kernel(**inputs):
    shared = host_prep(inputs)
    x = np.ascontiguousarray(np.asarray(inputs["L0_attn_x"], dtype=np.float32))
    nc = build_nc(debug=DEBUG)
    in_maps = [dict(shared, x=np.ascontiguousarray(x[b])) for b in range(B)]
    res = run_bass_kernel_spmd(nc, in_maps, core_ids=list(range(B)))
    return np.stack([np.asarray(res.results[b]["out"]).reshape(T, C) for b in range(B)])


# version 13 (final)
# speedup vs baseline: 1.0171x; 1.0056x over previous
import numpy as np

import concourse.bass as bass
import concourse.mybir as mybir
from concourse.bass_utils import run_bass_kernel_spmd

F = mybir.dt.float32
BF = mybir.dt.bfloat16
ALU = mybir.AluOpType
ACT = mybir.ActivationFunctionType
AX = mybir.AxisListType.X

B, T, C = 8, 2048, 1024
NG = 8
NQ = 4
TQ = T // NQ
NT = 16
NE = 8
HB = 32
NEG = -1.0e38
EPS = 1.0e-5

DEBUG = False


class Prog:
    ENGS = ("te", "ve", "sc", "sy", "gp")

    def __init__(self):
        self.stream = {e: [] for e in self.ENGS}
        self.sem_names = ["te", "ve", "sc"]
        self.cnt = {s: 0 for s in self.sem_names}
        self.waited = {e: {} for e in self.ENGS}

    def sem(self, name):
        if name not in self.sem_names:
            self.sem_names.append(name)
            self.cnt[name] = 0
        return name

    def op(self, eng, fn, inc=None, n=1):
        tok = None
        if inc is not None:
            self.cnt[inc] += n
            tok = (inc, self.cnt[inc])

        def run(h, sems, fn=fn, inc=inc, n=n):
            r = fn(h)
            if inc is not None:
                r.then_inc(sems[inc], n)
        self.stream[eng].append(run)
        return tok

    def wait(self, eng, tok):
        if tok is None:
            return
        s, v = tok
        if self.waited[eng].get(s, 0) >= v:
            return
        self.waited[eng][s] = v

        def run(h, sems, s=s, v=v):
            h.wait_ge(sems[s], v)
        self.stream[eng].append(run)

    def waits(self, eng, toks):
        for t in toks:
            self.wait(eng, t)

    def bar(self, sem):
        return (sem, self.cnt[sem])


def _tile_lhsT(WT):
    K, M = WT.shape
    return np.ascontiguousarray(
        WT.reshape(K // 128, 128, M // 128, 128).transpose(2, 1, 0, 3)
        .reshape(M // 128, 128, K))


def _cols(vec):
    return np.ascontiguousarray(vec.reshape(8, 128).T)


def host_prep(inputs):
    g = lambda k: np.ascontiguousarray(np.asarray(inputs[k], dtype=np.float32))
    kvr = np.stack([_tile_lhsT(g("L0_attn_Wk").T), _tile_lhsT(g("L0_attn_Wv").T),
                    _tile_lhsT(g("L0_attn_Wr").T)], axis=1)
    wck = _tile_lhsT(g("L0_attn_Wck").T)
    wcv = np.ascontiguousarray(g("L0_attn_Wcv").T.reshape(32, 128, 1024))
    wcr = np.ascontiguousarray(g("L0_attn_Wcr").T.reshape(8, 128, 1024))
    routr = np.ascontiguousarray(g("L0_ffn_router_w").T.reshape(8, 128, 8))
    w1 = g("L0_ffn_w1")
    w1t = np.stack([_tile_lhsT(w1[e].T) for e in range(NE)])
    w2 = g("L0_ffn_w2")
    w2t = np.stack([np.ascontiguousarray(w2[e].T.reshape(16, 128, 1024))
                    for e in range(NE)])
    cols = np.concatenate([
        _cols(g("L0_attn_" + k)) for k in
        ("w", "u", "rk", "rv", "rr", "crk", "crr",
         "ln1_g", "ln1_b", "ln2_g", "ln2_b")], axis=1)
    cols = np.ascontiguousarray(np.pad(cols, ((0, 0), (0, 8))))
    return {
        "kvrw": kvr, "wckw": wck, "wcvw": wcv, "wcrw": wcr, "routw": routr,
        "w1w": w1t, "w2w": w2t, "cols": cols,
        "ident": np.eye(128, dtype=np.float32),
        "ones": np.ones((128, 512), dtype=np.float32),
    }


CW, CU, CRK, CRV, CRR, CCRK, CCRR, CG1, CB1, CG2, CB2 = range(11)


def build_nc(debug=False):
    nc = bass.Bass("TRN2", target_bir_lowering=False)
    p = Prog()

    dx = nc.dram_tensor("x", [T, C], F, kind="ExternalInput").ap()
    dout = nc.dram_tensor("out", [T, C], F, kind="ExternalOutput").ap()
    dkvr = nc.dram_tensor("kvrw", [NG, 3, 128, 1024], F, kind="ExternalInput").ap()
    dwck = nc.dram_tensor("wckw", [HB, 128, 1024], F, kind="ExternalInput").ap()
    dwcv = nc.dram_tensor("wcvw", [HB, 128, 1024], F, kind="ExternalInput").ap()
    dwcr = nc.dram_tensor("wcrw", [8, 128, 1024], F, kind="ExternalInput").ap()
    drout = nc.dram_tensor("routw", [8, 128, 8], F, kind="ExternalInput").ap()
    dw1 = nc.dram_tensor("w1w", [NE, 32, 128, 1024], F, kind="ExternalInput").ap()
    dw2 = nc.dram_tensor("w2w", [NE, 16, 128, 1024], F, kind="ExternalInput").ap()
    dcols = nc.dram_tensor("cols", [128, 96], F, kind="ExternalInput").ap()
    dident = nc.dram_tensor("ident", [128, 128], F, kind="ExternalInput").ap()
    dones = nc.dram_tensor("ones", [128, 512], F, kind="ExternalInput").ap()
    taps = {}
    if debug:
        for nm, w in (("tap_x1", NT * 1024), ("tap_mix", NG * TQ),
                      ("tap_xfbf", NG * T), ("tap_comb", 128),
                      ("tap_a", NT * 1024), ("tap_b", NG * TQ),
                      ("tap_c", NT * 1024)):
            taps[nm] = nc.dram_tensor(nm, [128, w], F, kind="ExternalOutput").ap()

    ctx = []
    def ent(x):
        ctx.append(x)
        return x.__enter__()

    TO = 256
    X1 = ent(nc.sbuf_tensor("X1", [128, NT * 1024], F))
    VBp = [ent(nc.sbuf_tensor("VB%d" % i, [128, TO], F)) for i in range(2)]
    EKBp = [ent(nc.sbuf_tensor("EKB%d" % i, [128, TO], F)) for i in range(2)]
    ABp = [ent(nc.sbuf_tensor("AB%d" % i, [128, TO], F)) for i in range(2)]
    T1p = [ent(nc.sbuf_tensor("T1_%d" % i, [128, TO], F)) for i in range(2)]
    T2p = [ent(nc.sbuf_tensor("T2_%d" % i, [128, TO], F)) for i in range(2)]
    T3p = [ent(nc.sbuf_tensor("T3_%d" % i, [128, TO], F)) for i in range(2)]
    RBp = [ent(nc.sbuf_tensor("RB%d" % i, [128, TO], F)) for i in range(2)]
    YBp = [ent(nc.sbuf_tensor("YB%d" % i, [128, TO], F)) for i in range(2)]
    PENp = [ent(nc.sbuf_tensor("PEN%d" % i, [128, TO + 4], F)) for i in range(2)]
    AABp = [ent(nc.sbuf_tensor("AAB%d" % i, [128, TO + 4], F)) for i in range(2)]
    SHB = ent(nc.sbuf_tensor("SHB", [128, NG * 132], F))
    XT0 = ent(nc.sbuf_tensor("XT0", [128, 1024], F))
    XT1 = ent(nc.sbuf_tensor("XT1", [128, 1024], F))
    XTILEp = [ent(nc.sbuf_tensor("XTILE%d" % i, [128, 1024], F)) for i in range(2)]
    SCR = ent(nc.sbuf_tensor("SCR", [128, 1024], F))
    SCR2 = ent(nc.sbuf_tensor("SCR2", [128, 1024], F))
    VCB = ent(nc.sbuf_tensor("VCB", [128, 1024], F))
    CS = ent(nc.sbuf_tensor("CS", [128, 16], F))
    COLS = ent(nc.sbuf_tensor("COLS", [128, 96], F))
    DCOL = ent(nc.sbuf_tensor("DCOL", [128, 64], F))
    IDENT = ent(nc.sbuf_tensor("IDENT", [128, 128], F))
    ROUTW = ent(nc.sbuf_tensor("ROUTW", [128, 64], F))
    CMB = ent(nc.sbuf_tensor("CMB", [128, 128], F))
    SM8 = ent(nc.sbuf_tensor("SM8", [128, 64], F))
    KSCR = ent(nc.sbuf_tensor("KSCR", [128, 256], F))

    BFA = ent(nc.sbuf_tensor("BFA", [128, 45056], BF))
    MIXA = BFA[:, 0:4096]
    MIXB = BFA[:, 4096:8192]
    MIXC = BFA[:, 8192:12288]
    KVRW = [BFA[:, 12288 + 3072 * i:12288 + 3072 * (i + 1)] for i in range(2)]
    WCRW = BFA[:, 12288 + 6144:12288 + 6144 + 8192]
    WCKW = [BFA[:, 26624 + 4096 * i:26624 + 4096 * (i + 1)] for i in range(2)]
    WCVW = [BFA[:, 34816 + 4096 * i:34816 + 4096 * (i + 1)] for i in range(2)]
    KCB = BFA[:, 43008 + 0:43008 + 0]
    KCB = BFA[:, 8192 + 0:8192 + 8192]
    XFBF = BFA[:, 0:16384]
    YBF = BFA[:, 16384:32768]
    W1G = [BFA[:, 32768 + 2048 * i:32768 + 2048 * (i + 1)] for i in range(2)]
    W1U = [BFA[:, 36864 + 2048 * i:36864 + 2048 * (i + 1)] for i in range(2)]
    W2W = [BFA[:, 40960 + 2048 * i:40960 + 2048 * (i + 1)] for i in range(2)]

    PSA = ent(nc.psum_tensor("PSA", [128, 1024], F))
    PSB = ent(nc.psum_tensor("PSB", [128, 1024], F))
    PSC = ent(nc.psum_tensor("PSC", [128, 1024], F))
    PSD = ent(nc.psum_tensor("PSD", [128, 1024], F))
    PS = (PSA, PSB, PSC, PSD)

    col = lambda t, j: t[:, j:j + 1]

    def bc0(colap, n):
        return bass.AP(colap.tensor, colap.offset, [[colap.ap[0][0], 128], [0, n]])
    EW = lambda g: col(DCOL, g)
    EU = lambda g: col(DCOL, 8 + g)
    PP0 = lambda g: col(DCOL, 16 + g)
    AAC = lambda g: col(DCOL, 24 + g)
    MCp = lambda par: col(DCOL, 32 + 6 * par)
    MNp = lambda par: col(DCOL, 33 + 6 * par)
    C0p = lambda par: col(DCOL, 34 + 6 * par)
    C1p = lambda par: col(DCOL, 35 + 6 * par)
    KMp = lambda par: col(DCOL, 36 + 6 * par)
    MC, MN, C0, C1 = MCp(0), MNp(0), C0p(0), C1p(0)
    ccol = lambda name, j: col(COLS, 8 * name + j)
    xtile = lambda tt: X1[:, 1024 * tt:1024 * (tt + 1)]

    p.sem("dcn")
    p.op("sy", lambda h: h.dma_start(COLS[:, :], dcols), inc="dcn", n=16)
    p.op("sy", lambda h: h.dma_start(IDENT[:, :], dident), inc="dcn", n=16)
    t_const = p.op("sy", lambda h: h.dma_start(
        ROUTW[:, :], bass.AP(drout.tensor, 0, [[8, 128], [128 * 8, 8], [1, 8]])), inc="dcn", n=16)
    t_cols = t_id = t_ones = t_rout = t_const
    p.wait("sc", t_const)
    t_ew = p.op("sc", lambda h: h.activation(DCOL[:, 0:8], COLS[:, 8 * CW:8 * CW + 8], ACT.Exp), inc="sc")
    t_eu = p.op("sc", lambda h: h.activation(DCOL[:, 8:16], COLS[:, 8 * CU:8 * CU + 8], ACT.Exp), inc="sc")
    t_pp0 = p.op("ve", lambda h: h.memset(DCOL[:, 16:24], NEG), inc="ve")
    t_eps = p.op("ve", lambda h: h.memset(DCOL[:, 56:57], EPS), inc="ve")
    t_aac = p.op("ve", lambda h: h.memset(DCOL[:, 24:32], 0.0), inc="ve")
    t_shb = p.op("ve", lambda h: h.memset(SHB[:, :], 0.0), inc="ve")
    p.waits("ve", [t_ew, t_eu, t_id, t_ones, t_rout])

    last_sq = [None, None]

    psfree = {
        (i, hh): None for i in range(4) for hh in range(2)
    }

    def transpose_to(slot_ti, slot_half, slot_pos, src_ap, extra=()):
        slot = PS[slot_ti][:, 512 * slot_half + 128 * slot_pos:
                           512 * slot_half + 128 * slot_pos + 128]
        p.waits("te", [psfree[(slot_ti, slot_half)]] + list(extra))
        t = p.op("te", lambda h, slot=slot: h.transpose(slot, src_ap, IDENT[:, :]), inc="te")
        return slot, t

    mix_tok = [None] * NQ
    x1_tok = [None] * NT
    xt_free = [(None, None), (None, None)]
    shb_tok = [None] * NG
    xtile_free = [None, None]
    shb_tok = [None] * NG
    xtile_free = [None, None]

    def runz(*gens, skew=0):
        gens = [g for g in gens if g is not None]
        if skew and len(gens) > 1:
            for j, g in enumerate(list(gens)):
                for _ in range((len(gens) - 1 - j) * skew):
                    try:
                        next(g)
                    except StopIteration:
                        gens.remove(g)
                        break
        while gens:
            for g in list(gens):
                try:
                    next(g)
                except StopIteration:
                    gens.remove(g)

    def g_ln(src_ap, dst_ap, par, dep_ve=(), dep_sc=(), out=None):
        b = 8 * par
        mu_ap, r_ap = col(CS, b + 5), col(CS, b + 6)
        p.waits("ve", list(dep_ve))
        p.waits("sc", list(dep_sc) + [last_sq[par]])
        t_s = p.op("ve", lambda h: h.reduce_sum(col(CS, b + 0), src_ap, axis=AX), inc="ve")
        yield
        t_q = p.op("sc", lambda h: h.activation(dst_ap, src_ap, ACT.Square,
                                                accum_out=col(CS, b + 1)), inc="sc")
        last_sq[par] = t_q
        yield
        p.waits("ve", [t_s, t_q])
        t_mu = p.op("ve", lambda h: h.tensor_scalar(mu_ap, col(CS, b + 0), 1.0 / C, None, ALU.mult), inc="ve")
        yield
        p.wait("ve", t_mu)
        t_m2 = p.op("ve", lambda h: h.tensor_tensor(col(CS, b + 2), mu_ap, mu_ap, ALU.mult), inc="ve")
        yield
        p.wait("ve", t_m2)
        t_va = p.op("ve", lambda h: h.scalar_tensor_tensor(
            col(CS, b + 3), col(CS, b + 1), 1.0 / C, col(CS, b + 2), ALU.mult, ALU.subtract), inc="ve")
        yield
        p.wait("sc", t_va)
        t_ln = p.op("sc", lambda h: h.activation(col(CS, b + 4), col(CS, b + 3), ACT.Ln, bias=col(DCOL, 56)), inc="sc")
        yield
        p.wait("sc", t_ln)
        t_r = p.op("sc", lambda h: h.activation(r_ap, col(CS, b + 4), ACT.Exp, scale=-0.5), inc="sc")
        yield
        p.wait("ve", t_r)
        t_d = p.op("ve", lambda h: h.tensor_scalar(dst_ap, src_ap, mu_ap, r_ap,
                                                   ALU.subtract, ALU.mult), inc="ve")
        if out is not None:
            out["d"] = t_d
            out["q"] = t_q

    def g_p12tile(tt):
        q, tl = tt // 4, tt % 4
        par = tt % 2
        xt = XT0 if par == 0 else XT1
        p.waits("sy", list(xt_free[par]))
        t_x = p.op("sy", lambda h, tt=tt, xt=xt: h.dma_start(
            xt[:, :], dx[128 * tt:128 * (tt + 1), :]), inc=p.sem("dx%d" % par), n=16)
        yield
        o1 = {}
        for _ in g_ln(xt[:, :], xtile(tt), par, dep_ve=[t_x, t_shb], dep_sc=[t_x], out=o1):
            yield
        t_xln, t_sq = o1["d"], o1["q"]
        xt_free[par] = (t_xln, t_sq)
        x1_tok[tt] = t_xln
        XTILE = XTILEp[par]
        o2 = {}
        for _ in g_ln(xtile(tt), XTILE[:, :], par, dep_ve=[t_xln, xtile_free[par]],
                      dep_sc=[t_xln, xtile_free[par]], out=o2):
            yield
        t_xa = o2["d"]
        last = None
        for g in range(NG):
            slot, t_tr = transpose_to(0, par, g % 4, XTILE[:, 128 * g:128 * (g + 1)],
                                      extra=[t_xa])
            xtile_free[par] = t_tr
            yield
            sh = SHB[:, 132 * g:132 * g + 132]
            dscr = SCR2[:, 128 * (4 * par + g % 4):128 * (4 * par + g % 4) + 128]
            p.waits("ve", [t_tr, shb_tok[g]])
            t_ap = p.op("ve", lambda h, sh=sh, slot=slot, g=g: h.tensor_scalar(
                sh[:, 1:129], slot, ccol(CG1, g), ccol(CB1, g), ALU.mult, ALU.add), inc="ve")
            psfree[(0, par)] = t_ap
            yield
            p.wait("ve", t_ap)
            t_db = p.op("ve", lambda h, sh=sh, dscr=dscr: h.tensor_tensor(
                dscr, sh[:, 1:129], sh[:, 0:128], ALU.subtract), inc="ve")
            yield
            p.wait("ve", t_db)
            t_mx = None
            for mi, (dstT, rc) in enumerate(((MIXA, CRK), (MIXB, CRV), (MIXC, CRR))):
                dst = dstT[:, 512 * g + 128 * tl:512 * g + 128 * (tl + 1)]
                t_mx = p.op("ve", lambda h, dst=dst, rc=rc, sh=sh, g=g, dscr=dscr: h.scalar_tensor_tensor(
                    dst, dscr, ccol(rc, g), sh[:, 0:128], ALU.mult, ALU.add), inc="ve")
                yield
            p.wait("ve", t_mx)
            t_bd = p.op("ve", lambda h, sh=sh: h.tensor_copy(sh[:, 0:1], sh[:, 128:129]), inc="ve")
            shb_tok[g] = t_bd
            yield
            last = t_bd
        if mix_tok[q] is None or last[1] > mix_tok[q][1]:
            mix_tok[q] = last

    def kvr_load(i, g, dep=None):
        p.wait("gp", dep)
        return p.op("gp", lambda h, i=i, g=g: h.dma_start(
            KVRW[i], bass.AP(dkvr.tensor, g * 3 * 128 * 1024,
                             [[1024, 128], [128 * 1024, 3], [1, 1024]])), inc=p.sem("kvr%d" % i), n=16)

    kvr_dma = [None, None]
    kvr_free = [None, None]

    def g_p3(ii, iters):
        q, oj, g = iters[ii]
        par = ii % 2
        slot = KVRW[par]
        t_w = kvr_dma[par]
        ob = TO * oj
        mix = mix_tok[q]
        MC, MN, C0, C1, KM = MCp(par), MNp(par), C0p(par), C1p(par), KMp(par)
        EKB, VB, RB, YB = EKBp[par], VBp[par], RBp[par], YBp[par]
        T1, T2, T3, AB = T1p[par], T2p[par], T3p[par], ABp[par]
        PEN, AAB = PENp[par], AABp[par]
        psl = lambda t: t[:, 512 * par:512 * par + TO]
        p.waits("te", [mix, t_w, psfree[(1, par)]])
        tk = None
        for ck in range(NG):
            tk = p.op("te", lambda h, ck=ck, slot=slot, par=par, ob=ob: h.matmul(
                PSB[:, 512 * par:512 * par + TO], slot[:, 128 * ck:128 * (ck + 1)],
                MIXA[:, 512 * ck + ob:512 * ck + ob + TO], start=(ck == 0), stop=(ck == 7)),
                inc="te" if ck == 7 else None)
        yield
        p.wait("te", psfree[(2, par)])
        tv = None
        for ck in range(NG):
            tv = p.op("te", lambda h, ck=ck, slot=slot, par=par, ob=ob: h.matmul(
                PSC[:, 512 * par:512 * par + TO], slot[:, 1024 + 128 * ck:1024 + 128 * (ck + 1)],
                MIXB[:, 512 * ck + ob:512 * ck + ob + TO], start=(ck == 0), stop=(ck == 7)),
                inc="te" if ck == 7 else None)
        yield
        p.wait("te", psfree[(3, par)])
        tr = None
        for ck in range(NG):
            tr = p.op("te", lambda h, ck=ck, slot=slot, par=par, ob=ob: h.matmul(
                PSD[:, 512 * par:512 * par + TO], slot[:, 2048 + 128 * ck:2048 + 128 * (ck + 1)],
                MIXC[:, 512 * ck + ob:512 * ck + ob + TO], start=(ck == 0), stop=(ck == 7)),
                inc="te" if ck == 7 else None)
        kvr_free[par] = tr
        if ii + 2 < len(iters):
            kvr_dma[par] = kvr_load(par, iters[ii + 2][2], dep=tr)
        yield
        p.wait("ve", tk)
        t_km = p.op("ve", lambda h: h.reduce_max(KM, psl(PSB), axis=AX), inc="ve")
        yield
        p.wait("ve", t_km)
        t_M = p.op("ve", lambda h, g=g: h.tensor_tensor(MC, KM, PP0(g), ALU.max), inc="ve")
        yield
        p.wait("ve", t_M)
        t_MN = p.op("ve", lambda h: h.tensor_scalar(MN, MC, -1.0, None, ALU.mult), inc="ve")
        yield
        p.waits("sc", [tk, t_MN])
        t_ek = p.op("sc", lambda h: h.activation(EKB[:, :], psl(PSB), ACT.Exp, bias=MN), inc="sc")
        psfree[(1, par)] = t_ek
        yield
        p.wait("sc", tv)
        t_vev = p.op("sc", lambda h: h.copy(VB[:, :], psl(PSC)), inc="sc")
        psfree[(2, par)] = t_vev
        yield
        p.wait("sc", tr)
        t_r1 = p.op("sc", lambda h: h.activation(RB[:, :], psl(PSD), ACT.Exp, scale=-1.0), inc="sc")
        psfree[(3, par)] = t_r1
        yield
        p.wait("ve", t_r1)
        t_r2 = p.op("ve", lambda h: h.tensor_scalar(T3[:, :], RB[:, :], 1.0, None, ALU.add), inc="ve")
        yield
        p.wait("sc", t_r2)
        t_r3 = p.op("sc", lambda h: h.activation(T2[:, :], T3[:, :], ACT.Ln), inc="sc")
        yield
        p.wait("sc", t_r3)
        t_r4 = p.op("sc", lambda h: h.activation(RB[:, :], T2[:, :], ACT.Exp, scale=-1.0), inc="sc")
        yield
        p.waits("ve", [t_ek, t_ew, t_eu])
        t_d1 = p.op("ve", lambda h, g=g: h.tensor_scalar(T1[:, :], EKB[:, :], EW(g), None, ALU.mult), inc="ve")
        yield
        p.wait("sc", t_d1)
        t_pi = p.op("sc", lambda h, g=g: h.activation(PEN[:, 0:1], PP0(g), ACT.Exp, bias=MN), inc="sc")
        yield
        p.waits("ve", [t_d1, t_pi])
        t_ps = p.op("ve", lambda h, g=g: h.tensor_tensor_scan(
            PEN[:, 1:TO + 1], bc0(EW(g), TO), T1[:, :], PEN[:, 0:1], ALU.mult, ALU.add), inc="ve")
        yield
        p.wait("sc", t_ps)
        t_lp = p.op("sc", lambda h: h.activation(C1, PEN[:, TO:TO + 1], ACT.Ln), inc="sc")
        yield
        p.wait("ve", t_lp)
        t_pp = p.op("ve", lambda h, g=g: h.tensor_tensor(PP0(g), C1, MC, ALU.add), inc="ve")
        yield
        t_em = p.op("ve", lambda h: h.tensor_tensor(T2[:, :], PEN[:, 0:TO], EKB[:, :], ALU.max), inc="ve")
        yield
        p.wait("sc", t_em)
        t_l1 = p.op("sc", lambda h: h.activation(T3[:, :], T2[:, :], ACT.Ln), inc="sc")
        yield
        p.wait("sc", t_l1)
        t_iv = p.op("sc", lambda h: h.activation(T2[:, :], T3[:, :], ACT.Exp, scale=-1.0), inc="sc")
        yield
        p.wait("ve", t_iv)
        t_al = p.op("ve", lambda h: h.tensor_tensor(AB[:, :], PEN[:, 0:TO], T2[:, :], ALU.mult), inc="ve")
        yield
        t_be = p.op("ve", lambda h: h.tensor_tensor(T1[:, :], EKB[:, :], T2[:, :], ALU.mult), inc="ve")
        yield
        p.waits("ve", [t_be, t_vev])
        t_bv = p.op("ve", lambda h: h.tensor_tensor(T3[:, :], T1[:, :], VB[:, :], ALU.mult), inc="ve")
        yield
        t_a0 = p.op("ve", lambda h, g=g: h.tensor_copy(AAB[:, 0:1], AAC(g)), inc="ve")
        yield
        p.waits("ve", [t_al, t_bv, t_a0])
        t_as = p.op("ve", lambda h: h.tensor_tensor_scan(
            AAB[:, 1:TO + 1], AB[:, :], T3[:, :], AAB[:, 0:1], ALU.mult, ALU.add), inc="ve")
        yield
        p.wait("ve", t_as)
        t_ac = p.op("ve", lambda h, g=g: h.tensor_copy(AAC(g), AAB[:, TO:TO + 1]), inc="ve")
        yield
        t_Et = p.op("ve", lambda h, g=g: h.tensor_scalar(T1[:, :], EKB[:, :], EU(g), None, ALU.mult), inc="ve")
        yield
        p.wait("ve", t_Et)
        t_n1 = p.op("ve", lambda h: h.tensor_tensor(T2[:, :], PEN[:, 0:TO], AAB[:, 0:TO], ALU.mult), inc="ve")
        yield
        t_n2 = p.op("ve", lambda h: h.tensor_tensor(EKB[:, :], T1[:, :], VB[:, :], ALU.mult), inc="ve")
        yield
        p.waits("ve", [t_n1, t_n2])
        t_nu = p.op("ve", lambda h: h.tensor_tensor(T2[:, :], T2[:, :], EKB[:, :], ALU.add), inc="ve")
        yield
        t_de = p.op("ve", lambda h: h.tensor_tensor(T1[:, :], PEN[:, 0:TO], T1[:, :], ALU.add), inc="ve")
        yield
        p.waits("sc", [t_de])
        t_l2 = p.op("sc", lambda h: h.activation(T3[:, :], T1[:, :], ACT.Ln), inc="sc")
        yield
        p.wait("sc", t_l2)
        t_rd = p.op("sc", lambda h: h.activation(T1[:, :], T3[:, :], ACT.Exp, scale=-1.0), inc="sc")
        yield
        p.waits("ve", [t_nu, t_rd, t_r4])
        t_y = p.op("ve", lambda h: h.tensor_tensor(YB[:, :], T2[:, :], T1[:, :], ALU.mult), inc="ve")
        yield
        p.wait("ve", t_y)
        t_sy = p.op("ve", lambda h: h.tensor_tensor(YB[:, :], YB[:, :], RB[:, :], ALU.mult), inc="ve")
        yield
        for b in range(2):
            tt = 2 * (2 * q + oj) + b
            slot2, t_tr = transpose_to(0, par, g % 4, YB[:, 128 * b:128 * (b + 1)],
                                       extra=[t_sy])
            yield
            dst = X1[:, 1024 * tt + 128 * g:1024 * tt + 128 * (g + 1)]
            p.waits("ve", [t_tr, x1_tok[tt]])
            t_x1 = p.op("ve", lambda h, dst=dst, slot2=slot2: h.tensor_tensor(
                dst, dst, slot2, ALU.add), inc="ve")
            psfree[(0, par)] = t_x1
            x1_tok[tt] = t_x1
            yield

    iters = [(q, oj, g) for q in range(NQ) for oj in range(2) for g in range(NG)]
    kvr_dma[0] = kvr_load(0, iters[0][2])
    kvr_dma[1] = kvr_load(1, iters[1][2])
    for q in range(NQ):
        runz(g_p12tile(4 * q + 0), g_p12tile(4 * q + 1), skew=10)
        runz(g_p12tile(4 * q + 2), g_p12tile(4 * q + 3), skew=10)
        base = q * 16
        for j in range(8):
            runz(g_p3(base + 2 * j, iters), g_p3(base + 2 * j + 1, iters), skew=7)

    if debug:
        bv = p.bar("ve")
        p.wait("sy", bv)
        t_tap = p.op("sy", lambda h: h.dma_start(taps["tap_a"], X1[:, :]), inc=p.sem("dtp"), n=16)
        p.wait("ve", t_tap)
        p.wait("gp", bv)
        t_tap = p.op("gp", lambda h: h.dma_start(taps["tap_b"], MIXA[:, :]), inc=p.sem("dtg"), n=16)
        p.wait("ve", t_tap)
        p.wait("te", t_tap)

    p.wait("ve", p.bar("ve"))
    t_shb2 = p.op("ve", lambda h: h.memset(SHB[:, :], 0.0), inc="ve")
    mix2_tok = [None] * NQ

    def g_p4tile(tt):
        q, tl = tt // 4, tt % 4
        par = tt % 2
        xt = XT0 if par == 0 else XT1
        p.waits("sy", list(xt_free[par]))
        t_x = p.op("sy", lambda h, tt=tt, xt=xt: h.dma_start(
            xt[:, :], dx[128 * tt:128 * (tt + 1), :]), inc=p.sem("dx%d" % par), n=16)
        yield
        XTILE = XTILEp[par]
        o = {}
        for _ in g_ln(xtile(tt), XTILE[:, :], par,
                      dep_ve=[t_shb2, x1_tok[tt], xtile_free[par]],
                      dep_sc=[x1_tok[tt], xtile_free[par]], out=o):
            yield
        t_xb, t_sq = o["d"], o["q"]
        p.waits("ve", [t_xb, t_x])
        t_x2a = p.op("ve", lambda h, tt=tt, xt=xt: h.tensor_tensor(
            xtile(tt), xtile(tt), xt[:, :], ALU.add), inc="ve")
        x1_tok[tt] = t_x2a
        xt_free[par] = (t_x2a, t_sq)
        yield
        last = None
        for g in range(NG):
            slot, t_tr = transpose_to(0, par, g % 4, XTILE[:, 128 * g:128 * (g + 1)],
                                      extra=[t_xb])
            xtile_free[par] = t_tr
            yield
            sh = SHB[:, 132 * g:132 * g + 132]
            dscr = SCR2[:, 128 * (4 * par + g % 4):128 * (4 * par + g % 4) + 128]
            p.waits("ve", [t_tr, shb_tok[g]])
            t_ap = p.op("ve", lambda h, sh=sh, slot=slot, g=g: h.tensor_scalar(
                sh[:, 1:129], slot, ccol(CG2, g), ccol(CB2, g), ALU.mult, ALU.add), inc="ve")
            psfree[(0, par)] = t_ap
            yield
            p.wait("ve", t_ap)
            t_db = p.op("ve", lambda h, sh=sh, dscr=dscr: h.tensor_tensor(
                dscr, sh[:, 1:129], sh[:, 0:128], ALU.subtract), inc="ve")
            yield
            p.wait("ve", t_db)
            t_mx = None
            for dstT, rc in ((MIXA, CCRK), (MIXB, CCRR)):
                dst = dstT[:, 512 * g + 128 * tl:512 * g + 128 * (tl + 1)]
                t_mx = p.op("ve", lambda h, dst=dst, rc=rc, sh=sh, g=g, dscr=dscr: h.scalar_tensor_tensor(
                    dst, dscr, ccol(rc, g), sh[:, 0:128], ALU.mult, ALU.add), inc="ve")
                yield
            p.wait("ve", t_mx)
            t_bd = p.op("ve", lambda h, sh=sh: h.tensor_copy(sh[:, 0:1], sh[:, 128:129]), inc="ve")
            shb_tok[g] = t_bd
            yield
            last = t_bd
        if mix2_tok[q] is None or last[1] > mix2_tok[q][1]:
            mix2_tok[q] = last

    def emit_p4(q):
        runz(g_p4tile(4 * q + 0), g_p4tile(4 * q + 1), skew=10)
        runz(g_p4tile(4 * q + 2), g_p4tile(4 * q + 3), skew=10)

    t_wcr = p.op("gp", lambda h: h.dma_start(
        WCRW, bass.AP(dwcr.tensor, 0, [[1024, 128], [128 * 1024, 8], [1, 1024]])), inc=p.sem("wcr"), n=16)

    def wseq_load(dst_ring, i, dram, hb4, dep=None, tag=""):
        p.wait("gp", dep)
        return p.op("gp", lambda h, i=i, hb4=hb4: h.dma_start(
            dst_ring[i], bass.AP(dram.tensor, hb4 * 4 * 128 * 1024,
                                 [[1024, 128], [128 * 1024, 4], [1, 1024]])),
            inc=p.sem("%s%d" % (tag, i)), n=16)

    wck_dma = [None, None]
    wcv_dma = [None, None]
    wck_free = [None, None]
    wcv_free = [None, None]
    kc_tok = [None] * HB
    vcb_free = None

    def emit_p5(oct_):
        nonlocal vcb_free
        q, oj = oct_ // 2, oct_ % 2
        mix = mix2_tok[q]
        ob = 256 * oj
        wck_dma[0] = wseq_load(WCKW, 0, dwck, 0, dep=wck_free[0], tag='wck')
        for hb4 in range(8):
            i = hb4 % 2
            if hb4 + 1 < 8:
                wck_dma[(i + 1) % 2] = wseq_load(WCKW, (i + 1) % 2, dwck, hb4 + 1,
                                                 dep=wck_free[(i + 1) % 2], tag='wck')
            p.waits("te", [mix, wck_dma[i]])
            for hj in range(4):
                hblk = 4 * hb4 + hj
                half = hblk % 2
                p.wait("te", psfree[(2, half)])
                t_mm = None
                for ck in range(NG):
                    t_mm = p.op("te", lambda h, hj=hj, ck=ck, half=half, i=i, ob=ob: h.matmul(
                        PSC[:, 512 * half + 0:512 * half + 256],
                        WCKW[i][:, 1024 * hj + 128 * ck:1024 * hj + 128 * (ck + 1)],
                        MIXA[:, 512 * ck + ob:512 * ck + ob + 256],
                        start=(ck == 0), stop=(ck == 7)), inc="te" if ck == 7 else None)
                p.wait("sc", t_mm)
                t_rl = p.op("sc", lambda h, half=half: h.activation(
                    KSCR[:, :], PSC[:, 512 * half:512 * half + 256], ACT.Relu), inc="sc")
                p.waits("ve", [t_rl, t_mm])
                t_kc = p.op("ve", lambda h, hblk=hblk, half=half: h.tensor_tensor(
                    KCB[:, 256 * hblk:256 * (hblk + 1)], KSCR[:, :],
                    PSC[:, 512 * half:512 * half + 256], ALU.mult), inc="ve")
                p.wait("sc", t_kc)
                kc_tok[hblk] = t_kc
                psfree[(2, half)] = t_kc
                if hj == 3:
                    wck_free[i] = t_mm
        wcv_dma[0] = wseq_load(WCVW, 0, dwcv, 0, dep=wcv_free[0], tag='wcv')
        t_vc = [None, None]
        for hb4 in range(8):
            i = hb4 % 2
            if hb4 + 1 < 8:
                wcv_dma[(i + 1) % 2] = wseq_load(WCVW, (i + 1) % 2, dwcv, hb4 + 1,
                                                 dep=wcv_free[(i + 1) % 2], tag='wcv')
            p.wait("te", wcv_dma[i])
            for hj in range(4):
                hblk = 4 * hb4 + hj
                p.wait("te", kc_tok[hblk])
                if hblk == 0:
                    p.waits("te", [psfree[(0, 0)], psfree[(0, 1)],
                                   psfree[(1, 0)], psfree[(1, 1)]])
                t_last = None
                for t2 in range(2):
                    for dh in range(2):
                        t_last = p.op("te", lambda h, hblk=hblk, t2=t2, dh=dh, i=i, hj=hj: h.matmul(
                            PS[t2][:, 512 * dh:512 * (dh + 1)],
                            KCB[:, 256 * hblk + 128 * t2:256 * hblk + 128 * (t2 + 1)],
                            WCVW[i][:, 1024 * hj + 512 * dh:1024 * hj + 512 * (dh + 1)],
                            start=(hblk == 0), stop=(hblk == HB - 1)),
                            inc="te" if (hblk == HB - 1 and t2 == 1 and dh == 1) or (hj == 3 and t2 == 1 and dh == 1) else None)
                t_vc[1] = t_last
            if hb4 == 7:
                wcv_free[i] = t_last
            else:
                wcv_free[i] = t_last
        t_vc_done = t_vc[1]
        for t2 in range(2):
            tt = 2 * oct_ + t2
            p.waits("ve", [t_vc_done, vcb_free])
            t_ev = p.op("ve", lambda h, t2=t2: h.tensor_copy(VCB[:, :], PS[t2][:, :]), inc="ve")
            psfree[(t2, 0)] = t_ev
            psfree[(t2, 1)] = t_ev
            p.waits("te", [mix, t_wcr, psfree[(3, 0)], psfree[(3, 1)]])
            t_cr = None
            for ck in range(NG):
                for dh in range(2):
                    t_cr = p.op("te", lambda h, ck=ck, dh=dh, t2=t2, ob=ob: h.matmul(
                        PSD[:, 512 * dh:512 * (dh + 1)],
                        MIXB[:, 512 * ck + ob + 128 * t2:512 * ck + ob + 128 * (t2 + 1)],
                        WCRW[:, 1024 * ck + 512 * dh:1024 * ck + 512 * (dh + 1)],
                        start=(ck == 0), stop=(ck == 7)),
                        inc="te" if (ck == 7 and dh == 1) else None)
            p.wait("sc", t_cr)
            t_s1 = p.op("sc", lambda h: h.activation(SCR[:, :], PSD[:, :], ACT.Exp, scale=-1.0), inc="sc")
            psfree[(3, 0)] = t_s1
            psfree[(3, 1)] = t_s1
            p.wait("ve", t_s1)
            t_s2 = p.op("ve", lambda h: h.tensor_scalar(SCR2[:, :], SCR[:, :], 1.0, None, ALU.add), inc="ve")
            p.wait("sc", t_s2)
            t_s3 = p.op("sc", lambda h: h.activation(SCR[:, :], SCR2[:, :], ACT.Ln), inc="sc")
            p.wait("sc", t_s3)
            t_s4 = p.op("sc", lambda h: h.activation(SCR2[:, :], SCR[:, :], ACT.Exp, scale=-1.0), inc="sc")
            p.wait("sc", t_s4)
            p.waits("ve", [t_s4, t_ev])
            t_m = p.op("ve", lambda h: h.tensor_tensor(VCB[:, :], SCR2[:, :], VCB[:, :], ALU.mult), inc="ve")
            p.waits("ve", [t_m, x1_tok[tt]])
            t_x2 = p.op("ve", lambda h, tt=tt: h.tensor_tensor(
                xtile(tt), xtile(tt), VCB[:, :], ALU.add), inc="ve")
            x1_tok[tt] = t_x2
            vcb_free = t_x2


    for q in range(NQ):
        emit_p4(q)
        if debug and q == NQ - 1:
            bv = p.bar("ve")
            p.wait("gp", bv)
            t_tapm = p.op("gp", lambda h: h.dma_start(taps["tap_mix"], MIXA[:, :]), inc=p.sem("dtg"), n=16)
            p.wait("ve", t_tapm)
            p.wait("te", t_tapm)
        emit_p5(2 * q)
        emit_p5(2 * q + 1)
    t_p5te_glob = p.bar("te")

    if debug:
        bv = p.bar("ve")
        p.wait("sy", bv)
        t_tap = p.op("sy", lambda h: h.dma_start(taps["tap_c"], X1[:, :]), inc=p.sem("dtp"), n=16)
        p.wait("ve", t_tap)

    tr6 = {0: p.bar("ve"), 1: p.bar("ve")}
    prev_router = {0: None, 1: None}
    strip_bufs = (SCR2, VCB)
    cmb_tok = [None] * NT

    def g_p6tile(tt):
        par = tt % 2
        sb = lambda j: col(DCOL, 44 + 6 * par + j)
        smb = 32 * par
        STRIP = strip_bufs[par]
        XTILE = XTILEp[par]
        o = {}
        for _ in g_ln(xtile(tt), XTILE[:, :], par,
                      dep_ve=[x1_tok[tt], xtile_free[par]],
                      dep_sc=[x1_tok[tt], xtile_free[par]], out=o):
            yield
        t_xf = o["d"]
        strip_done = None
        for g in range(NG):
            ti = 1 + g // 4
            slot, t_tr = transpose_to(ti, par, g % 4, XTILE[:, 128 * g:128 * (g + 1)],
                                      extra=[t_xf, prev_router[par]])
            xtile_free[par] = t_tr
            yield
            p.waits("ve", [t_tr, prev_router[par]])
            t_c1 = p.op("ve", lambda h, g=g, slot=slot, STRIP=STRIP: h.tensor_copy(
                STRIP[:, 128 * g:128 * (g + 1)], slot), inc="ve")
            psfree[(ti, par)] = t_c1
            yield
            p.wait("ve", t_c1)
            t_c2 = p.op("ve", lambda h, g=g, tt=tt, STRIP=STRIP: h.tensor_copy(
                XFBF[:, 2048 * g + 128 * tt:2048 * g + 128 * (tt + 1)],
                STRIP[:, 128 * g:128 * (g + 1)]), inc="ve")
            strip_done = t_c2
            yield
        gps = PSA[:, 512 * par:512 * par + 8]
        p.waits("te", [strip_done, psfree[(0, par)]])
        t_g = None
        for g in range(NG):
            t_g = p.op("te", lambda h, g=g, gps=gps, STRIP=STRIP: h.matmul(
                gps, STRIP[:, 128 * g:128 * (g + 1)], ROUTW[:, 8 * g:8 * (g + 1)],
                start=(g == 0), stop=(g == 7)), inc="te" if g == 7 else None)
        prev_router[par] = t_g
        yield
        p.waits("ve", [t_g])
        t_m1 = p.op("ve", lambda h, gps=gps: h.reduce_max(sb(0), gps, axis=AX), inc="ve")
        yield
        p.wait("ve", t_m1)
        t_m1n = p.op("ve", lambda h: h.tensor_scalar(sb(1), sb(0), -1.0, None, ALU.mult), inc="ve")
        yield
        p.waits("sc", [t_g, t_m1n])
        t_eg = p.op("sc", lambda h, gps=gps: h.activation(SM8[:, smb:smb + 8], gps, ACT.Exp, bias=sb(1)), inc="sc")
        psfree[(0, par)] = t_eg
        yield
        p.wait("ve", t_eg)
        t_ss = p.op("ve", lambda h: h.reduce_sum(sb(2), SM8[:, smb:smb + 8], axis=AX), inc="ve")
        yield
        p.wait("sc", t_ss)
        t_ls = p.op("sc", lambda h: h.activation(sb(3), sb(2), ACT.Ln), inc="sc")
        yield
        p.wait("sc", t_ls)
        t_is = p.op("sc", lambda h: h.activation(sb(4), sb(3), ACT.Exp, scale=-1.0), inc="sc")
        yield
        p.wait("ve", t_is)
        t_pr = p.op("ve", lambda h: h.tensor_scalar(SM8[:, smb + 8:smb + 16], SM8[:, smb:smb + 8],
                                                    sb(4), None, ALU.mult), inc="ve")
        yield
        p.wait("ve", t_pr)
        t_mm = p.op("ve", lambda h: h.reduce_max(sb(5), SM8[:, smb + 8:smb + 16], axis=AX), inc="ve")
        yield
        p.wait("ve", t_mm)
        t_eq = p.op("ve", lambda h: h.tensor_scalar(SM8[:, smb + 16:smb + 24], SM8[:, smb + 8:smb + 16],
                                                    sb(5), None, ALU.is_equal), inc="ve")
        yield
        p.wait("ve", t_eq)
        t_mr = p.op("ve", lambda h: h.scalar_tensor_tensor(SM8[:, smb + 16:smb + 24], SM8[:, smb + 16:smb + 24],
                                                           NEG, SM8[:, smb + 8:smb + 16], ALU.mult, ALU.add), inc="ve")
        yield
        p.wait("ve", t_mr)
        t_m2 = p.op("ve", lambda h: h.reduce_max(sb(5), SM8[:, smb + 16:smb + 24], axis=AX), inc="ve")
        yield
        p.wait("ve", t_m2)
        t_msk = p.op("ve", lambda h: h.tensor_scalar(SM8[:, smb + 24:smb + 32], SM8[:, smb + 8:smb + 16],
                                                     sb(5), None, ALU.is_ge), inc="ve")
        yield
        p.wait("ve", t_msk)
        t_cb = p.op("ve", lambda h, tt=tt: h.tensor_tensor(
            CMB[:, 8 * tt:8 * (tt + 1)], SM8[:, smb + 8:smb + 16], SM8[:, smb + 24:smb + 32], ALU.mult), inc="ve")
        cmb_tok[tt] = t_cb

    for tp2 in range(8):
        runz(g_p6tile(2 * tp2), g_p6tile(2 * tp2 + 1))

    if debug:
        bv = p.bar("ve")
        p.wait("gp", bv)
        t_tap = p.op("gp", lambda h: h.dma_start(taps["tap_xfbf"], XFBF), inc=p.sem("dtg"), n=16)
        p.wait("ve", t_tap)
        p.wait("te", t_tap)
        p.wait("sy", bv)
        t_tap2 = p.op("sy", lambda h: h.dma_start(taps["tap_comb"], CMB[:, :]), inc=p.sem("dtp"), n=16)
        p.wait("ve", t_tap2)

    def w1_load(i, e, fp, dep):
        p.wait("gp", dep)
        tg = p.op("gp", lambda h, i=i, e=e, fp=fp: h.dma_start(
            W1G[i], bass.AP(dw1.tensor, (e * 32 + 2 * fp) * 128 * 1024,
                            [[1024, 128], [128 * 1024, 2], [1, 1024]])), inc=p.sem("w1g%d" % i), n=16)
        tu = p.op("gp", lambda h, i=i, e=e, fp=fp: h.dma_start(
            W1U[i], bass.AP(dw1.tensor, (e * 32 + 16 + 2 * fp) * 128 * 1024,
                            [[1024, 128], [128 * 1024, 2], [1, 1024]])), inc=p.sem("w1u%d" % i), n=16)
        return (tg, tu)

    def w2_load(i, e, dh, fq, dep):
        p.wait("gp", dep)
        return p.op("gp", lambda h, i=i, e=e, dh=dh, fq=fq: h.dma_start(
            W2W[i], bass.AP(dw2.tensor, (e * 16 + 4 * fq) * 128 * 1024 + dh * 512,
                            [[1024, 128], [128 * 1024, 4], [1, 512]])), inc=p.sem("w2s%d" % i), n=16)

    PSg = (PSA, PSC)
    PSu = (PSB, PSD)
    gate_free = [p.bar("ve"), p.bar("ve")]
    pso_free = [p.bar("ve")] * 8
    t_p5te = t_p5te_glob
    w1_dma = [None, None]
    w1_free = [t_p5te, t_p5te]
    w2_dma = [None, None]
    w2_free = [t_p5te, t_p5te]

    seq = [(e, th) for e in range(NE) for th in range(2)]
    w1_dma[0] = w1_load(0, 0, 0, w1_free[0])
    for si, (e, th) in enumerate(seq):
        ybf_tok = [None] * 16
        for fp in range(8):
            i = fp % 2
            if fp + 1 < 8:
                w1_dma[(i + 1) % 2] = w1_load((i + 1) % 2, e, fp + 1, w1_free[(i + 1) % 2])
            elif si + 1 < len(seq):
                ne, nth = seq[si + 1]
                if nth == 0:
                    w1_dma[(i + 1) % 2] = w1_load((i + 1) % 2, ne, 0, w1_free[(i + 1) % 2])
                else:
                    w1_dma[(i + 1) % 2] = w1_load((i + 1) % 2, ne, 0, w1_free[(i + 1) % 2])
            p.waits("te", list(w1_dma[i]))
            for fj in range(2):
                f = 2 * fp + fj
                pp = f % 2
                p.wait("te", gate_free[pp])
                t_gu = None
                for ck in range(NG):
                    for sh2 in range(2):
                        ro = 2048 * ck + 1024 * th + 512 * sh2
                        p.op("te", lambda h, pp=pp, fj=fj, ck=ck, sh2=sh2, i=i, ro=ro: h.matmul(
                            PSg[pp][:, 512 * sh2:512 * (sh2 + 1)],
                            W1G[i][:, 1024 * fj + 128 * ck:1024 * fj + 128 * (ck + 1)],
                            XFBF[:, ro:ro + 512], start=(ck == 0), stop=(ck == 7)))
                        t_gu = p.op("te", lambda h, pp=pp, fj=fj, ck=ck, sh2=sh2, i=i, ro=ro: h.matmul(
                            PSu[pp][:, 512 * sh2:512 * (sh2 + 1)],
                            W1U[i][:, 1024 * fj + 128 * ck:1024 * fj + 128 * (ck + 1)],
                            XFBF[:, ro:ro + 512], start=(ck == 0), stop=(ck == 7)),
                            inc="te" if (ck == 7 and sh2 == 1) else None)
                p.wait("sc", t_gu)
                p.wait("sc", gate_free[1 - pp])
                t_si_ = p.op("sc", lambda h, pp=pp: h.activation(SCR[:, :], PSg[pp][:, :], ACT.Silu), inc="sc")
                p.waits("ve", [t_si_, t_gu])
                t_y = p.op("ve", lambda h, f=f, pp=pp: h.tensor_tensor(
                    YBF[:, 1024 * f:1024 * (f + 1)], SCR[:, :], PSu[pp][:, :], ALU.mult), inc="ve")
                ybf_tok[f] = t_y
                gate_free[pp] = t_y
                if fj == 1:
                    w1_free[i] = t_gu
        for dh in range(2):
            w2_dma[0] = w2_load(0, e, dh, 0, w2_free[0])
            t_o = [None] * 8
            for fq in range(4):
                i2 = fq % 2
                if fq + 1 < 4:
                    w2_dma[(i2 + 1) % 2] = w2_load((i2 + 1) % 2, e, dh, fq + 1, w2_free[(i2 + 1) % 2])
                p.wait("te", w2_dma[i2])
                for fj in range(4):
                    f = 4 * fq + fj
                    p.waits("te", [ybf_tok[f]])
                    if f == 0:
                        p.waits("te", [ybf_tok[14], ybf_tok[15]])
                        p.waits("te", pso_free)
                    for tl in range(8):
                        bank = PS[tl // 2]
                        t_o[tl] = p.op("te", lambda h, f=f, tl=tl, i2=i2, fj=fj, bank=bank: h.matmul(
                            bank[:, 512 * (tl % 2):512 * (tl % 2 + 1)],
                            YBF[:, 1024 * f + 128 * tl:1024 * f + 128 * (tl + 1)],
                            W2W[i2][:, 512 * fj:512 * (fj + 1)],
                            start=(f == 0), stop=(f == 15)),
                            inc="te" if f == 15 or fj == 3 else None)
                if fq == 3:
                    w2_free[i2] = t_o[7]
                else:
                    w2_free[i2] = t_o[7] if t_o[7] else p.bar("te")
            for tl in range(8):
                tt = 8 * th + tl
                bank = PS[tl // 2]
                p.waits("ve", [t_o[tl], cmb_tok[tt]])
                dst = X1[:, 1024 * tt + 512 * dh:1024 * tt + 512 * (dh + 1)]
                t_evo = p.op("ve", lambda h, dst=dst, bank=bank, tl=tl, tt=tt, e=e: h.scalar_tensor_tensor(
                    dst, bank[:, 512 * (tl % 2):512 * (tl % 2 + 1)],
                    col(CMB, 8 * tt + e), dst, ALU.mult, ALU.add), inc="ve")
                pso_free[tl] = t_evo
                x1_tok[tt] = t_evo
            gate_free[0] = pso_free[0]
            gate_free[1] = pso_free[4]
            for tl in range(8):
                p.wait("te", pso_free[tl])

    if debug:
        bv = p.bar("ve")
        p.wait("sy", bv)
        p.op("sy", lambda h: h.dma_start(taps["tap_x1"], X1[:, :]), inc=p.sem("dtp"), n=16)

    p.sem("dou")
    for tt in range(NT):
        p.wait("sy", x1_tok[tt])
        p.op("sy", lambda h, tt=tt: h.dma_start(
            dout[128 * tt:128 * (tt + 1), :], xtile(tt)), inc="dou", n=16)
    final_dou = p.cnt["dou"]
    p.stream["sy"].append(lambda h, sems, v=final_dou: h.wait_ge(sems["dou"], v))
    if debug:
        for snm in ("dtg", "dtp"):
            if snm in p.cnt:
                p.stream["sy"].append(
                    lambda h, sems, s2=snm, v=p.cnt[snm]: h.wait_ge(sems[s2], v))

    for s, v in p.cnt.items():
        assert v < 30000, (s, v)

    sems = {s2: ent(nc.semaphore("sem_" + s2)) for s2 in p.sem_names}

    with nc.Block() as blk:
        @blk.sync
        def _(h):
            for fn in p.stream["sy"]:
                fn(h, sems)

        @blk.gpsimd
        def _(h):
            for fn in p.stream["gp"]:
                fn(h, sems)

        @blk.scalar
        def _(h):
            for fn in p.stream["sc"]:
                fn(h, sems)

        @blk.vector
        def _(h):
            for fn in p.stream["ve"]:
                fn(h, sems)

        @blk.tensor
        def _(h):
            for fn in p.stream["te"]:
                fn(h, sems)

    for c in reversed(ctx):
        c.__exit__(None, None, None)
    return nc


def kernel(**inputs):
    shared = host_prep(inputs)
    x = np.ascontiguousarray(np.asarray(inputs["L0_attn_x"], dtype=np.float32))
    nc = build_nc(debug=DEBUG)
    in_maps = [dict(shared, x=np.ascontiguousarray(x[b])) for b in range(B)]
    res = run_bass_kernel_spmd(nc, in_maps, core_ids=list(range(B)))
    return np.stack([np.asarray(res.results[b]["out"]).reshape(T, C) for b in range(B)])
